# version 21
# baseline (speedup 1.0000x reference)
import contextlib

import numpy as np
import ml_dtypes

import concourse.bass as bass
import concourse.mybir as mybir
from concourse.bass_utils import run_bass_kernel_spmd

F32 = mybir.dt.float32
BF16 = mybir.dt.bfloat16
AF = mybir.ActivationFunctionType
ALU = mybir.AluOpType
AX = mybir.AxisListType

B, T, C, H, D = 16, 2048, 1024, 8, 128
I, E = 4096, 4
NCORE = 8
SPC = B // NCORE
NT = T // 128
NCH = C // 128
NIC = I // 128
TB = 512
STW = SPC * T
NTB = T // TB
INV_SQRT_D = float(np.float32(1.0) / np.float32(np.sqrt(np.float64(D))))
SHIFT_MAIN = -4.0
SHIFT_GEO = -12.0

_CACHE = {}


class Ctx:

    def __init__(self, nc, stack):
        self.nc = nc
        self.stack = stack
        self.sems = {}
        self.cnt = {}

    def sem(self, name):
        if name not in self.sems:
            self.sems[name] = self.stack.enter_context(self.nc.semaphore(name))
            self.cnt[name] = 0
        return self.sems[name]

    def inc(self, name, k=1):
        self.sem(name)
        self.cnt[name] += k
        return self.cnt[name]

    def val(self, name):
        self.sem(name)
        return self.cnt[name]


class Phase:

    ENGS = ("sync", "gpsimd", "vector", "scalar", "tensor")

    def __init__(self, ctx, name):
        self.ctx = ctx
        self.name = name
        self.q = {e: [] for e in self.ENGS}

    def on(self, eng, fn):
        self.q[eng].append(fn)

    def dma(self, eng, out, in_, sem, slow=False):
        v = self.ctx.inc(sem, 16)
        s = self.ctx.sems[sem]
        self.on(eng, lambda e, out=out, in_=in_, s=s: e.dma_start(
            out=out, in_=in_, allow_slow_non_contiguous=slow).then_inc(s, 16))
        return v

    def wait(self, eng, sem, val):
        s = self.ctx.sem(sem)
        self.on(eng, lambda e, s=s, val=val: e.wait_ge(s, val))

    def run(self, eng, fn, sem=None):
        if sem is None:
            self.on(eng, fn)
            return None
        v = self.ctx.inc(sem, 1)
        s = self.ctx.sems[sem]
        self.on(eng, lambda e, fn=fn, s=s: fn(e).then_inc(s, 1))
        return v

    def rund(self, eng, fn, sem=None):
        self.on(eng, fn)
        if sem is None:
            self.on(eng, lambda e: e.drain())
            return None
        v = self.ctx.inc(sem, 1)
        s = self.ctx.sems[sem]
        self.on(eng, lambda e, s=s: e.drain().then_inc(s, 1))
        return v

    def build(self):
        nc = self.ctx.nc
        with nc.Block(self.name) as blk:
            for eng in self.ENGS:
                thunks = self.q[eng]
                if not thunks:
                    continue
                getattr(blk, eng)(lambda e, thunks=thunks: [f(e) for f in thunks])


def build_program():
    nc = bass.Bass("TRN2", target_bir_lowering=False)
    dt = nc.dram_tensor

    x_tok = dt("x_tok", [SPC, T, C], F32, kind="ExternalInput").ap()
    x_fm = dt("x_fm", [SPC, C, T], F32, kind="ExternalInput").ap()
    wqT = dt("wqT", [C, C], F32, kind="ExternalInput").ap()
    wkT = dt("wkT", [C, C], F32, kind="ExternalInput").ap()
    wvT = dt("wvT", [C, C], F32, kind="ExternalInput").ap()
    woT = dt("woT", [C, C], F32, kind="ExternalInput").ap()
    ln_g = dt("ln_g", [C], F32, kind="ExternalInput").ap()
    ln_b = dt("ln_b", [C], F32, kind="ExternalInput").ap()
    hscale = dt("hscale", [H], F32, kind="ExternalInput").ap()
    dirsT = dt("dirsT", [3, H], F32, kind="ExternalInput").ap()
    wr = dt("wr", [C, E], F32, kind="ExternalInput").ap()
    w1 = dt("w1", [E, C, I], F32, kind="ExternalInput").ap()
    b1 = dt("b1", [E, I], F32, kind="ExternalInput").ap()
    wo2 = dt("wo2", [I, C], F32, kind="ExternalInput").ap()
    bo = dt("bo", [C], F32, kind="ExternalInput").ap()
    gam = dt("gam", [C], F32, kind="ExternalInput").ap()
    bet = dt("bet", [C], F32, kind="ExternalInput").ap()
    ropec = dt("ropec", [D, T], F32, kind="ExternalInput").ap()
    ropes = dt("ropes", [D, T], F32, kind="ExternalInput").ap()
    rswapT = dt("rswapT", [128, 128], BF16, kind="ExternalInput").ap()
    tri_in = dt("tri_c", [128, 128], BF16, kind="ExternalInput").ap()
    ident_b = dt("ident_b", [128, 128], BF16, kind="ExternalInput").ap()
    ident_f = dt("ident_f", [128, 128], F32, kind="ExternalInput").ap()
    ones_in = dt("ones_in", [1, 128], F32, kind="ExternalInput").ap()
    sel4f_in = dt("sel4f_in", [4, 4 * 128], F32, kind="ExternalInput").ap()
    sel4b_in = dt("sel4b_in", [4, 4 * 128], BF16, kind="ExternalInput").ap()

    out = dt("out", [SPC, T, C], F32, kind="ExternalOutput").ap()

    w1bf = dt("w1bf", [E, C, I], BF16, kind="Internal").ap()
    wobf = dt("wobf", [I, C], BF16, kind="Internal").ap()

    with contextlib.ExitStack() as st:
        En = st.enter_context
        ctx = Ctx(nc, st)

        stats = En(nc.sbuf_tensor("stats", [128, 224], F32))
        srows4 = En(nc.sbuf_tensor("srows4", [4, STW], F32))
        masksr = En(nc.sbuf_tensor("masksr", [4, SPC * T], BF16))
        sel4f = En(nc.sbuf_tensor("sel4f", [4, 4 * 128], F32))
        sel4b = En(nc.sbuf_tensor("sel4b", [4, 4 * 128], BF16))
        g_col = En(nc.sbuf_tensor("g_col", [128, NCH], F32))
        b_col = En(nc.sbuf_tensor("b_col", [128, NCH], F32))
        hs_r = En(nc.sbuf_tensor("hs_r", [1, H], F32))
        omh_r = En(nc.sbuf_tensor("omh_r", [1, H], F32))
        dirs_bf = En(nc.sbuf_tensor("dirs_bf", [3, H], BF16))
        dirs_rep = En(nc.sbuf_tensor("dirs_rep", [3, H * 128], BF16))
        tri = En(nc.sbuf_tensor("tri", [128, 128], BF16))
        rsw = En(nc.sbuf_tensor("rsw", [128, 128], BF16))
        id_b = En(nc.sbuf_tensor("id_b", [128, 128], BF16))
        ones_cb = En(nc.sbuf_tensor("ones_cb", [128, 1], BF16))
        ones_cf = En(nc.sbuf_tensor("ones_cf", [128, 1], F32))
        ones_rf = En(nc.sbuf_tensor("ones_rf", [1, 128], F32))
        ones_rb = En(nc.sbuf_tensor("ones_rb", [1, 128], BF16))
        ones3f = En(nc.sbuf_tensor("ones3f", [3, 128], F32))
        dirsT_sb = En(nc.sbuf_tensor("dirsT_sb", [3, H], F32))
        wr_sb = En(nc.sbuf_tensor("wr_sb", [128, NCH * E], F32))
        sum_r = En(nc.sbuf_tensor("sum_r", [1, E], F32))
        sum_b = En(nc.sbuf_tensor("sum_b", [128, E], F32))
        consts = En(nc.sbuf_tensor("consts", [128, 8], F32))
        sc = En(nc.sbuf_tensor("sc", [128, 8], F32))
        lgt = En(nc.sbuf_tensor("lgt", [128, E], F32))
        oh_t = En(nc.sbuf_tensor("oh_t", [128, E], BF16))
        kp_all = En(nc.sbuf_tensor("kp_all", [128, H * NT], F32))
        rows_t = En(nc.sbuf_tensor("rows_t", [128, 128], F32))

        CONST_VALS = [INV_SQRT_D, SHIFT_MAIN, SHIFT_GEO, 1e-5, 1e-12]
        for i, v in enumerate(CONST_VALS):
            nc.const_aps.register(F32, v, consts[:, i : i + 1])

        ph = Phase(ctx, "init")
        for i, v in enumerate(CONST_VALS):
            ph.run("vector", lambda e, i=i, v=v: e.memset(consts[:, i : i + 1], v))
        ph.run("vector", lambda e: e.memset(ones_cb[:, :], 1.0))
        ph.run("vector", lambda e: e.memset(ones_cf[:, :], 1.0))
        ph.dma("sync", g_col[:, :], ln_g.rearrange("(a p) -> p a", p=128), "ld0", slow=True)
        ph.dma("sync", b_col[:, :], ln_b.rearrange("(a p) -> p a", p=128), "ld0", slow=True)
        ph.dma("sync", hs_r[0:1, :], hscale.rearrange("(a h) -> a h", a=1), "ld0")
        ph.dma("sync", tri[:, :], tri_in, "ld0")
        ph.dma("sync", rsw[:, :], rswapT, "ld0")
        ph.dma("sync", id_b[:, :], ident_b, "ld0")
        ph.dma("sync", ones_rf[0:1, :], ones_in, "ld0")
        ph.dma("sync", wr_sb[:, :].rearrange("p (a e) -> p a e", a=NCH),
               wr.rearrange("(a p) e -> p a e", p=128), "ld0", slow=True)
        ph.dma("gpsimd", dirs_bf[0:3, :], dirsT, "ldg")
        ph.dma("sync", dirsT_sb[0:3, :], dirsT, "ld0")
        ph.dma("sync", sel4f[0:4, :], sel4f_in, "ld0")
        ph.dma("sync", sel4b[0:4, :], sel4b_in, "ld0")
        v_ld0 = ctx.val("ld0")
        v_ldg = ctx.val("ldg")
        ph.run("vector", lambda e: e.memset(ones_rb[0:1, :], 1.0))
        ph.run("vector", lambda e: e.memset(ones3f[0:3, :], 1.0))
        ph.wait("vector", "ld0", v_ld0)
        for h in range(H):
            ph.run("vector", lambda e, h=h: e.tensor_single_scalar(
                dirs_rep[0:3, h * 128 : (h + 1) * 128], ones3f[0:3, :],
                dirsT_sb[0:3, h : h + 1], ALU.mult))
        ph.rund("vector", lambda e: e.tensor_scalar(omh_r[0:1, :], hs_r[0:1, :], -1.0, 1.0,
                                                    ALU.mult, ALU.add))
        for e_ in range(E):
            for half in range(2):
                sl = slice(half * 512, (half + 1) * 512)
                ph.dma("gpsimd", w1bf[e_, sl, :], w1[e_, sl, :], "ldg")
        for q in range(4):
            sl = slice(q * 1024, (q + 1) * 1024)
            ph.dma("gpsimd", wobf[sl, :], wo2[sl, :], "ldg")
        ph.wait("tensor", "ld0", v_ld0)
        ph.wait("vector", "vI", 0)
        with nc.psum_tensor("psI", [1, E], F32) as psI, \
                nc.psum_tensor("psI2", [128, 1024], F32) as psI2:
            for a in range(NCH):
                ph.run("tensor", lambda e, a=a, psI=psI: e.matmul(
                    psI[0:1, :], ones_cf[:, :], wr_sb[:, a * E : (a + 1) * E],
                    start=(a == 0), stop=(a == NCH - 1)),
                    sem="mmI" if a == NCH - 1 else None)
            ph.wait("scalar", "mmI", ctx.val("mmI"))
            ph.run("scalar", lambda e, psI=psI: e.copy(sum_r[0:1, :], psI[0:1, :]), sem="evI")
            ph.wait("tensor", "evI", ctx.val("evI"))
            bt = [(sum_b[:, 0:E], sum_r[0:1, 0:E])]
            for bi, (dst, srcrow) in enumerate(bt):
                n = dst.shape[1]
                ph.run("tensor", lambda e, srcrow=srcrow, n=n, psI2=psI2: e.matmul(
                    psI2[:, 0:n], ones_rf[0:1, :], srcrow, start=True, stop=True), sem="mmI")
                ph.wait("scalar", "mmI", ctx.val("mmI"))
                ph.run("scalar", lambda e, dst=dst, n=n, psI2=psI2: e.copy(dst, psI2[:, 0:n]),
                       sem="evI")
                ph.wait("tensor", "evI", ctx.val("evI"))
            ph.wait("sync", "evI", ctx.val("evI"))
            ph.wait("sync", "ldg", ctx.val("ldg"))
            ph.build()

        ph = Phase(ctx, "statsA")
        with contextlib.ExitStack() as pst:
            Ep = pst.enter_context
            xt_buf = Ep(nc.sbuf_tensor("xt_buf", [128, 2 * C], F32))
            dump = Ep(nc.sbuf_tensor("dumpA", [128, 1024], BF16))
            psT = Ep(nc.psum_tensor("psT", [128, 128], F32))
            for s in range(SPC):
                for tt in range(NT):
                    k = s * NT + tt
                    par = k % 2
                    xt = xt_buf[:, par * C : (par + 1) * C]
                    if k >= 2:
                        ph.wait("sync", "stA", ctx.val("stA") - 1)
                    ph.dma("sync", xt, x_tok[s, tt * 128 : (tt + 1) * 128, :], "ldA")
                    ph.wait("vector", "ldA", ctx.val("ldA"))
                    ph.rund("vector", lambda e, xt=xt: e.reduce_sum(sc[:, 0:1], xt, axis=AX.X))
                    ph.rund("vector", lambda e, xt=xt: e.scalar_tensor_tensor(
                        dump[:, :], xt, 1.0, xt, ALU.mult, ALU.mult, accum_out=sc[:, 1:2]))
                    mu = stats[:, 160 + k : 161 + k]
                    ph.rund("vector", lambda e, mu=mu: e.tensor_scalar_mul(mu, sc[:, 0:1], 1.0 / C))
                    ph.rund("vector", lambda e: e.tensor_scalar_mul(sc[:, 1:2], sc[:, 1:2], 1.0 / C))
                    ph.rund("vector", lambda e, mu=mu: e.tensor_mul(sc[:, 2:3], mu, mu))
                    ph.rund("vector", lambda e: e.tensor_sub(sc[:, 2:3], sc[:, 1:2], sc[:, 2:3]),
                            sem="vA")
                    ph.wait("scalar", "vA", ctx.val("vA"))
                    r1 = stats[:, k : k + 1]
                    ph.rund("scalar", lambda e: e.activation(sc[:, 5:6], sc[:, 2:3], AF.Sqrt,
                                                             bias=1e-5), sem="sA")
                    ph.wait("vector", "sA", ctx.val("sA"))
                    ph.rund("vector", lambda e, r1=r1: e.reciprocal(r1, sc[:, 5:6]))
                    ph.rund("vector", lambda e, r1=r1: e.tensor_mul(sc[:, 3:4], r1, r1))
                    ph.rund("vector", lambda e: e.tensor_mul(sc[:, 3:4], sc[:, 2:3], sc[:, 3:4]),
                            sem="vA")
                    ph.wait("scalar", "vA", ctx.val("vA"))
                    ph.rund("scalar", lambda e: e.activation(sc[:, 5:6], sc[:, 3:4], AF.Sqrt,
                                                             bias=1e-5), sem="sA")
                    ph.wait("vector", "sA", ctx.val("sA"))
                    ph.rund("vector", lambda e: e.reciprocal(sc[:, 4:5], sc[:, 5:6]))
                    sa = stats[:, 64 + k : 65 + k]
                    ph.rund("vector", lambda e, r1=r1, sa=sa: e.tensor_mul(sa, r1, sc[:, 4:5]))
                    ph.rund("vector", lambda e, r1=r1, mu=mu, k=k: e.tensor_mul(
                        stats[:, 32 + k : 33 + k], mu, r1))
                    ph.rund("vector", lambda e, sa=sa, mu=mu, k=k: e.tensor_mul(
                        stats[:, 96 + k : 97 + k], mu, sa))
                    ph.rund("vector", lambda e, k=k: e.tensor_scalar_mul(
                        stats[:, 128 + k : 129 + k], stats[:, 32 + k : 33 + k], -1.0), sem="stA")
            ph.wait("tensor", "stA", ctx.val("stA"))
            idf_sb = Ep(nc.sbuf_tensor("idf_sb", [128, 128], F32))
            ph.dma("sync", idf_sb[:, :], ident_f, "ldA")
            ph.wait("tensor", "ldA", ctx.val("ldA"))
            ph.run("tensor", lambda e: e.transpose(psT[:, :], stats[:, 0:128], idf_sb[:, :]),
                   sem="mmT")
            ph.wait("scalar", "mmT", ctx.val("mmT"))
            ph.run("scalar", lambda e: e.copy(rows_t[:, :], psT[:, :]), sem="sA")
            ph.wait("sync", "sA", ctx.val("sA"))
            for j in range(4):
                ph.dma("sync",
                       srows4[j : j + 1, :].rearrange("p (k f) -> p k f", k=32),
                       rows_t[j * 32 : (j + 1) * 32, 0:128], "ldA")
            ph.wait("sync", "ldA", ctx.val("ldA"))
            ph.build()

        for s in range(SPC):
            with contextlib.ExitStack() as sst:
                Es = sst.enter_context
                zqA = Es(nc.sbuf_tensor(f"zqA{s}", [128, NCH * T], BF16))

                ph = Phase(ctx, f"c1_{s}")
                with contextlib.ExitStack() as pst:
                    Ep = pst.enter_context
                    xfm = Ep(nc.sbuf_tensor(f"xfm{s}", [128, NCH * TB], F32))
                    tmpf = Ep(nc.sbuf_tensor(f"tmpf{s}", [128, TB], F32))
                    psB = Ep(nc.psum_tensor(f"psB{s}", [128, 2 * TB], F32))
                    psR = Ep(nc.psum_tensor(f"psR{s}", [128, E], F32))
                    psM = Ep(nc.psum_tensor(f"psM{s}", [4, 128], BF16))
                    for tb in range(NTB):
                        t0 = tb * TB
                        if tb > 0:
                            ph.wait("tensor", "zC", ctx.val("zC"))
                        ph.run("tensor", lambda e, t0=t0, psB=psB: e.matmul(
                            psB[:, 0:TB], sel4f[0:4, 2 * 128 : 3 * 128],
                            srows4[0:4, s * T + t0 : s * T + t0 + TB],
                            start=True, stop=True))
                        ph.run("tensor", lambda e, t0=t0, psB=psB: e.matmul(
                            psB[:, TB : 2 * TB], sel4f[0:4, 3 * 128 : 4 * 128],
                            srows4[0:4, s * T + t0 : s * T + t0 + TB],
                            start=True, stop=True), sem="mmC")
                        if tb > 0:
                            ph.wait("sync", "zC", ctx.val("zC"))
                            ph.wait("sync", "vR", ctx.val("vR"))
                        for a in range(NCH):
                            ph.dma("sync", xfm[:, a * TB : (a + 1) * TB],
                                   x_fm[s, a * 128 : (a + 1) * 128, t0 : t0 + TB], "ldC")
                        ph.wait("vector", "ldC", ctx.val("ldC"))
                        ph.wait("vector", "mmC", ctx.val("mmC"))
                        for a in range(NCH):
                            xa = xfm[:, a * TB : (a + 1) * TB]
                            ph.run("vector", lambda e, xa=xa, psB=psB: e.tensor_mul(
                                tmpf[:, :], xa, psB[:, 0:TB]))
                            ph.run("vector", lambda e, psB=psB: e.tensor_sub(
                                tmpf[:, :], tmpf[:, :], psB[:, TB : 2 * TB]))
                            ph.run("vector", lambda e, a=a, t0=t0: e.tensor_scalar(
                                zqA[:, a * T + t0 : a * T + t0 + TB], tmpf[:, :],
                                g_col[:, a : a + 1], b_col[:, a : a + 1], ALU.mult, ALU.add),
                                sem="zC" if a == NCH - 1 else None)
                        ph.wait("tensor", "ldC", ctx.val("ldC"))
                        for q in range(TB // 128):
                            tt = t0 // 128 + q
                            k = s * NT + tt
                            if tb + q > 0:
                                ph.wait("tensor", "vR", ctx.val("vR"))
                            for a in range(NCH):
                                ph.run("tensor", lambda e, a=a, q=q, psR=psR: e.matmul(
                                    psR[:, :], xfm[:, a * TB + q * 128 : a * TB + q * 128 + 128],
                                    wr_sb[:, a * E : (a + 1) * E],
                                    start=(a == 0), stop=(a == NCH - 1)),
                                    sem="mmR" if a == NCH - 1 else None)
                            ph.wait("vector", "mmR", ctx.val("mmR"))
                            if tb + q > 0:
                                ph.wait("vector", "sM", ctx.val("sM"))
                            ph.rund("vector", lambda e, k=k, psR=psR: e.tensor_single_scalar(
                                lgt[:, :], psR[:, :], stats[:, k : k + 1], ALU.mult))
                            ph.rund("vector", lambda e, k=k: e.scalar_tensor_tensor(
                                lgt[:, :], sum_b[:, :], stats[:, 128 + k : 129 + k], lgt[:, :],
                                ALU.mult, ALU.add))
                            ph.rund("vector", lambda e: e.reduce_max(sc[:, 6:7], lgt[:, :],
                                                                     axis=AX.X))
                            ph.rund("vector", lambda e: e.tensor_single_scalar(
                                oh_t[:, :], lgt[:, :], sc[:, 6:7], ALU.is_ge), sem="vR")
                            ph.wait("tensor", "vR", ctx.val("vR"))
                            if tb + q > 0:
                                ph.wait("tensor", "sM", ctx.val("sM"))
                            ph.run("tensor", lambda e, psM=psM: e.transpose(
                                psM[:, :], oh_t[:, :], id_b[:, :]), sem="mmM")
                            ph.wait("scalar", "mmM", ctx.val("mmM"))
                            ph.run("scalar", lambda e, tt=tt, psM=psM: e.copy(
                                masksr[0:4, s * T + tt * 128 : s * T + tt * 128 + 128],
                                psM[0:4, :]), sem="sM")
                    ph.wait("sync", "sM", ctx.val("sM"))
                    ph.wait("sync", "zC", ctx.val("zC"))
                    ph.build()

                qkv_st = contextlib.ExitStack()
                Qfm = qkv_st.enter_context(nc.sbuf_tensor(f"Qfm{s}", [128, H * T], BF16))
                Kfm = qkv_st.enter_context(nc.sbuf_tensor(f"Kfm{s}", [128, H * T], BF16))
                Vtk = qkv_st.enter_context(nc.sbuf_tensor(f"Vtk{s}", [128, NT * C], BF16))

                ph = Phase(ctx, f"c2_{s}")
                with contextlib.ExitStack() as pst:
                    Ep = pst.enter_context
                    wbuf = Ep(nc.sbuf_tensor(f"wbuf{s}", [128, NCH * C], BF16))
                    psP = Ep(nc.psum_tensor(f"psP{s}", [128, 4 * TB], F32))
                    rtmpq = Ep(nc.sbuf_tensor(f"rtmpq{s}", [128, T], BF16))
                    rtmpk = Ep(nc.sbuf_tensor(f"rtmpk{s}", [128, T], BF16))
                    ropec_bf = Ep(nc.sbuf_tensor(f"ropecb{s}", [D, T], BF16))
                    ropes_bf = Ep(nc.sbuf_tensor(f"ropesb{s}", [D, T], BF16))
                    ph.dma("gpsimd", ropec_bf[:, :], ropec, "ldg")
                    ph.dma("gpsimd", ropes_bf[:, :], ropes, "ldg")
                    grp = 0
                    evs = {}
                    for wi, (wsrc, dst) in enumerate(((wqT, Qfm), (wkT, Kfm), (wvT, Vtk))):
                        if wi > 0:
                            ph.wait("gpsimd", "mmP", ctx.val("mmP"))
                        ph.dma("gpsimd", wbuf[:, :].rearrange("p (a m) -> p a m", a=NCH),
                               wsrc.rearrange("(a p) m -> p a m", p=128), "ldW")
                        ph.wait("tensor", "ldW", ctx.val("ldW"))
                        specs = ([(mc, tb) for mc in range(NCH) for tb in range(NTB)] if wi < 2
                                 else [(tt, mh) for tt in range(NT) for mh in range(2)])
                        for u, v_ in specs:
                            ps = psP[:, (grp % 4) * TB : (grp % 4 + 1) * TB]
                            if grp >= 4:
                                ph.wait("tensor", "evP", evs[grp - 4])
                            for a in range(NCH):
                                if wi < 2:
                                    fn = lambda e, ps=ps, a=a, mc=u, tb=v_: e.matmul(
                                        ps, wbuf[:, a * C + mc * 128 : a * C + mc * 128 + 128],
                                        zqA[:, a * T + tb * TB : a * T + (tb + 1) * TB],
                                        start=(a == 0), stop=(a == NCH - 1))
                                else:
                                    fn = lambda e, ps=ps, a=a, tt=u, mh=v_: e.matmul(
                                        ps, zqA[:, a * T + tt * 128 : a * T + tt * 128 + 128],
                                        wbuf[:, a * C + mh * 512 : a * C + (mh + 1) * 512],
                                        start=(a == 0), stop=(a == NCH - 1))
                                ph.run("tensor", fn, sem="mmP" if a == NCH - 1 else None)
                            ph.wait("scalar", "mmP", ctx.val("mmP"))
                            if wi < 2:
                                dsl = dst[:, u * T + v_ * TB : u * T + (v_ + 1) * TB]
                            else:
                                dsl = Vtk[:, u * C + v_ * 512 : u * C + (v_ + 1) * 512]
                            ph.run("scalar", lambda e, ps=ps, dsl=dsl: e.copy(dsl, ps), sem="evP")
                            evs[grp] = ctx.val("evP")
                            grp += 1
                    vev = ctx.val("evP")
                    vldg = ctx.val("ldg")
                    ph.wait("tensor", "evP", vev)
                    ph.wait("scalar", "ldg", vldg)
                    ph.wait("vector", "ldg", vldg)
                    ph.wait("gpsimd", "ldg", vldg)
                    jobs = [(eng, dst, rtmp, h)
                            for h in range(H)
                            for eng, dst, rtmp in (("vector", Qfm, rtmpq),
                                                   ("gpsimd", Kfm, rtmpk))]
                    for ji, (eng, dst, rtmp, h) in enumerate(jobs):
                        q_ = dst[:, h * T : (h + 1) * T]
                        if ji > 0:
                            ph.wait("tensor", "sro", ctx.val("sro"))
                        for tq in range(4):
                            ph.run("tensor", lambda e, q_=q_, tq=tq: e.matmul(
                                psP[:, tq * TB : (tq + 1) * TB], rsw[:, :],
                                q_[:, tq * TB : (tq + 1) * TB], start=True, stop=True),
                                sem="mmRo" if tq == 3 else None)
                        ph.wait("scalar", "mmRo", ctx.val("mmRo"))
                        if ji >= 2:
                            ph.wait("scalar", "rp", ctx.val("rp") - 1)
                        ph.run("scalar", lambda e, rtmp=rtmp: e.copy(rtmp[:, :], psP[:, :]),
                               sem="sro")
                        ph.wait(eng, "sro", ctx.val("sro"))
                        ph.run(eng, lambda e, rtmp=rtmp: e.tensor_mul(
                            rtmp[:, :], rtmp[:, :], ropes_bf[:, :]))
                        ph.run(eng, lambda e, q_=q_: e.tensor_mul(q_, q_, ropec_bf[:, :]))
                        ph.run(eng, lambda e, q_=q_, rtmp=rtmp: e.tensor_add(
                            q_, q_, rtmp[:, :]), sem="rp")
                    ph.wait("sync", "rp", ctx.val("rp"))
                    ph.build()

                ph = Phase(ctx, f"d_{s}")
                with contextlib.ExitStack() as pst:
                    Ep = pst.enter_context
                    Ocomb = Ep(nc.sbuf_tensor(f"Oc{s}", [128, H * TB], BF16))
                    woT_sb = Ep(nc.sbuf_tensor(f"woTs{s}", [128, NCH * C], BF16))
                    ph.dma("gpsimd", woT_sb[:, :].rearrange("p (a m) -> p a m", a=NCH),
                           woT.rearrange("(a p) m -> p a m", p=128), "ldg")
                    ph.wait("tensor", "ldg", ctx.val("ldg"))
                    Pm = Ep(nc.sbuf_tensor(f"Pm{s}", [128, 3 * TB], BF16))
                    Pg = Ep(nc.sbuf_tensor(f"Pg{s}", [128, 3 * TB], BF16))
                    qpsb = Ep(nc.sbuf_tensor(f"qpsb{s}", [128, TB], BF16))
                    mixf = Ep(nc.sbuf_tensor(f"mixf{s}", [128, TB], BF16))
                    lrow = Ep(nc.sbuf_tensor(f"lrow{s}", [33, 2 * TB], BF16))
                    abrow = Ep(nc.sbuf_tensor(f"abrow{s}", [1, 2 * TB], F32))
                    absb = Ep(nc.sbuf_tensor(f"absb{s}", [128, 2 * TB], F32))
                    OMsb = Ep(nc.sbuf_tensor(f"OMsb{s}", [128, 2 * TB], BF16))
                    OGsb = Ep(nc.sbuf_tensor(f"OGsb{s}", [128, 2 * TB], BF16))
                    with nc.psum_tensor(f"psKP{s}", [128, NT], F32) as psKP:
                        for h in range(H):
                            if h > 0:
                                ph.wait("tensor", "evKP", ctx.val("evKP"))
                            for kt in range(NT):
                                ph.run("tensor", lambda e, h=h, kt=kt, psKP=psKP: e.matmul(
                                    psKP[:, kt : kt + 1],
                                    Kfm[0:3, h * T + kt * 128 : h * T + kt * 128 + 128],
                                    dirs_bf[0:3, h : h + 1],
                                    start=(kt == 0), stop=(kt == NT - 1)),
                                    sem="mmKP" if kt == NT - 1 else None)
                            ph.wait("vector", "mmKP", ctx.val("mmKP"))
                            ph.rund("vector", lambda e, h=h, psKP=psKP: e.tensor_copy(
                                kp_all[:, h * NT : (h + 1) * NT], psKP[:, :]), sem="evKP")
                        ph.wait("tensor", "evKP", ctx.val("evKP"))

                    psS_ = Ep(nc.psum_tensor(f"psS{s}", [128, 3 * TB], F32))
                    psQP = Ep(nc.psum_tensor(f"psQP{s}", [128, TB], F32))
                    psOM = Ep(nc.psum_tensor(f"psOM{s}", [128, TB], F32))
                    psOG = Ep(nc.psum_tensor(f"psOG{s}", [128, TB], F32))
                    psLM = Ep(nc.psum_tensor(f"psLM{s}", [1, TB], F32))
                    psLG = Ep(nc.psum_tensor(f"psLG{s}", [1, TB], F32))


                    def emit_mix_tail(dd, stage):
                        hq, hp, ev = dd[0], dd[1], dd[2]
                        arow = abrow[0:1, :]
                        lr = lrow[hp * 32 : hp * 32 + 1, :]
                        ab = absb[:, :]
                        if stage == 0:
                            ph.wait("vector", "evO", ev)
                            ph.rund("vector", lambda e, arow=arow, lr=lr: e.reciprocal(
                                arow[:, 0:TB], lr[:, 0:TB]))
                            ph.rund("vector", lambda e, arow=arow, lr=lr: e.reciprocal(
                                arow[:, TB : 2 * TB], lr[:, TB : 2 * TB]))
                            ph.rund("vector", lambda e, arow=arow, hq=hq: e.tensor_single_scalar(
                                arow[:, 0:TB], arow[:, 0:TB], omh_r[0:1, hq : hq + 1], ALU.mult))
                            ph.rund("vector", lambda e, arow=arow, hq=hq: e.tensor_single_scalar(
                                arow[:, TB : 2 * TB], arow[:, TB : 2 * TB],
                                hs_r[0:1, hq : hq + 1], ALU.mult), sem="vab")
                            dd_v = ctx.val("vab")
                            return dd + (dd_v,)
                        if stage == 1:
                            ph.wait("tensor", "sQP", ctx.val("sQP"))
                            ph.wait("tensor", "vab", dd[3])
                            ph.run("tensor", lambda e, arow=arow: e.matmul(
                                psQP[:, :], ones_rf[0:1, :], arow[0:1, 0:TB],
                                start=True, stop=True), sem="mmABa")
                            return dd + (ctx.val("mmABa"),)
                        if stage == 2:
                            ph.wait("scalar", "voc", ctx.val("voc"))
                            ph.wait("scalar", "mmABa", dd[4])
                            ph.run("scalar", lambda e, ab=ab: e.copy(ab[:, 0:TB], psQP[:, :]),
                                   sem="sabA")
                            return dd + (ctx.val("sabA"),)
                        if stage == 3:
                            ph.wait("tensor", "sabA", dd[5])
                            ph.run("tensor", lambda e, arow=arow: e.matmul(
                                psQP[:, :], ones_rf[0:1, :], arow[0:1, TB : 2 * TB],
                                start=True, stop=True), sem="mmABb")
                            return dd + (ctx.val("mmABb"),)
                        if stage == 4:
                            ph.wait("scalar", "mmABb", dd[6])
                            ph.run("scalar", lambda e, ab=ab: e.copy(ab[:, TB : 2 * TB],
                                                                     psQP[:, :]), sem="sabB")
                            return dd + (ctx.val("sabB"),)
                        ph.wait("vector", "sabB", dd[7])
                        om = OMsb[:, hp * TB : (hp + 1) * TB]
                        og = OGsb[:, hp * TB : (hp + 1) * TB]
                        oc = Ocomb[:, hq * TB : (hq + 1) * TB]
                        ph.run("vector", lambda e, om=om, ab=ab: e.tensor_mul(
                            mixf[:, :], om, ab[:, 0:TB]))
                        ph.run("vector", lambda e, og=og, ab=ab, oc=oc: e.tensor_mul(
                            oc, og, ab[:, TB : 2 * TB]))
                        ph.run("vector", lambda e, oc=oc: e.tensor_add(
                            oc, oc, mixf[:, :]), sem="voc")
                        return None

                    deferred = None
                    for qb in range(NTB):
                        nkt = 4 * qb + 4
                        for h in range(H):
                            hp = h % 2
                            if h == 0 and qb > 0:
                                ph.wait("tensor", "evA", ctx.val("evA"))
                            if deferred is not None:
                                deferred = emit_mix_tail(deferred, 0)
                            ph.wait("tensor", "sabB", ctx.val("sabB"))
                            ph.wait("tensor", "sm", ctx.val("sm"))
                            ph.run("tensor", lambda e, h=h, qb=qb: e.matmul(
                                psQP[:, :], dirs_rep[0:3, h * 128 : (h + 1) * 128],
                                Qfm[0:3, h * T + qb * TB : h * T + (qb + 1) * TB],
                                start=True, stop=True), sem="mmQP")
                            ph.wait("scalar", "mmQP", ctx.val("mmQP"))
                            ph.wait("scalar", "sg", ctx.val("sg"))
                            ph.run("scalar", lambda e: e.copy(qpsb[:, :], psQP[:, :]), sem="sQP")
                            prev_pv = ctx.val("mmPV")
                            prev_evo = ctx.val("evO")
                            sm_done, sg_done, pv_done, vd_done = {}, {}, {}, {}

                            def emit_S(kt):
                                par = kt % 3
                                pss = psS_[:, par * TB : (par + 1) * TB]
                                pmm = Pm[:, par * TB : (par + 1) * TB]
                                pgg = Pg[:, par * TB : (par + 1) * TB]
                                diag = kt >= 4 * qb
                                dlt = (kt - 4 * qb) * 128 if diag else 0
                                if kt >= 3:
                                    ph.wait("tensor", "sm", sm_done[kt - 3])
                                ph.run("tensor", lambda e, pss=pss, h=h, qb=qb, kt=kt: e.matmul(
                                    pss, Kfm[:, h * T + kt * 128 : h * T + kt * 128 + 128],
                                    Qfm[:, h * T + qb * TB : h * T + (qb + 1) * TB],
                                    start=True, stop=True), sem="mmS")
                                ph.wait("scalar", "mmS", ctx.val("mmS"))
                                ph.wait("scalar", "mmPV",
                                        pv_done[kt - 3] if kt >= 3 else prev_pv)
                                ph.run("scalar", lambda e, pmm=pmm, pss=pss, dlt=dlt: e.activation(
                                    pmm[:, dlt:TB], pss[:, dlt:TB], AF.Exp,
                                    bias=SHIFT_MAIN, scale=INV_SQRT_D), sem="sm")
                                sm_done[kt] = ctx.val("sm")
                                kps = kp_all[:, h * NT + kt : h * NT + kt + 1]
                                ph.run("scalar", lambda e, pgg=pgg, dlt=dlt, kps=kps: e.activation(
                                    pgg[:, dlt:TB], qpsb[:, dlt:TB], AF.Exp,
                                    bias=SHIFT_GEO, scale=kps), sem="sg")
                                sg_done[kt] = ctx.val("sg")
                                if diag:
                                    ph.wait("vector", "sm", sm_done[kt])
                                    ph.wait("vector", "sg", sg_done[kt])
                                    for pp in (pmm, pgg):
                                        if dlt > 0:
                                            ph.run("vector", lambda e, pp=pp, dlt=dlt: e.memset(
                                                pp[:, 0:dlt], 0.0))
                                        ph.run("vector", lambda e, pp=pp, dlt=dlt: e.tensor_mul(
                                            pp[:, dlt : dlt + 128], pp[:, dlt : dlt + 128],
                                            tri[:, :]), sem="vd")
                                    vd_done[kt] = ctx.val("vd")

                            def emit_PV(kt):
                                par = kt % 3
                                pmm = Pm[:, par * TB : (par + 1) * TB]
                                pgg = Pg[:, par * TB : (par + 1) * TB]
                                if kt in vd_done:
                                    ph.wait("tensor", "vd", vd_done[kt])
                                else:
                                    ph.wait("tensor", "sm", sm_done[kt])
                                    ph.wait("tensor", "sg", sg_done[kt])
                                if kt == 0:
                                    ph.wait("tensor", "evO", prev_evo)
                                vt = Vtk[:, kt * C + h * 128 : kt * C + (h + 1) * 128]
                                ph.run("tensor", lambda e, pmm=pmm, vt=vt, kt=kt, nkt=nkt: e.matmul(
                                    psOM[:, :], vt, pmm, start=(kt == 0), stop=(kt == nkt - 1)))
                                ph.run("tensor", lambda e, pmm=pmm, kt=kt, nkt=nkt: e.matmul(
                                    psLM[0:1, :], ones_cb[:, :], pmm,
                                    start=(kt == 0), stop=(kt == nkt - 1)))
                                ph.run("tensor", lambda e, pgg=pgg, vt=vt, kt=kt, nkt=nkt: e.matmul(
                                    psOG[:, :], vt, pgg, start=(kt == 0), stop=(kt == nkt - 1)))
                                ph.run("tensor", lambda e, pgg=pgg, kt=kt, nkt=nkt: e.matmul(
                                    psLG[0:1, :], ones_cb[:, :], pgg,
                                    start=(kt == 0), stop=(kt == nkt - 1)), sem="mmPV")
                                pv_done[kt] = ctx.val("mmPV")

                            dstage = [1 if deferred is not None else 6]

                            def adv(upto):
                                nonlocal deferred
                                while dstage[0] < upto:
                                    r = emit_mix_tail(deferred, dstage[0])
                                    if dstage[0] == 5:
                                        deferred = None
                                    else:
                                        deferred = r
                                    dstage[0] += 1

                            emit_S(0)
                            emit_S(1)
                            emit_S(2)
                            emit_PV(0)
                            for kt in range(3, nkt):
                                emit_S(kt)
                                if kt == 3:
                                    adv(3)
                                emit_PV(kt - 2)
                                if kt == 4:
                                    adv(6)
                            emit_PV(nkt - 2)
                            emit_PV(nkt - 1)
                            adv(6)

                            ph.wait("scalar", "voc", ctx.val("voc"))
                            ph.wait("scalar", "mmPV", ctx.val("mmPV"))
                            ph.run("scalar", lambda e, hp=hp: e.copy(
                                OMsb[:, hp * TB : (hp + 1) * TB], psOM[:, :]))
                            ph.run("scalar", lambda e, hp=hp: e.copy(
                                OGsb[:, hp * TB : (hp + 1) * TB], psOG[:, :]))
                            ph.run("scalar", lambda e, hp=hp: e.copy(
                                lrow[hp * 32 : hp * 32 + 1, 0:TB], psLM[0:1, :]))
                            ph.rund("scalar", lambda e, hp=hp: e.copy(
                                lrow[hp * 32 : hp * 32 + 1, TB : 2 * TB], psLG[0:1, :]),
                                sem="evO")
                            deferred = (h, hp, ctx.val("evO"))

                        for stg in range(5):
                            deferred = emit_mix_tail(deferred, stg)
                        emit_mix_tail(deferred, 5)
                        deferred = None
                        ph.wait("tensor", "voc", ctx.val("voc"))
                        for q in range(TB // 128):
                            if qb + q > 0:
                                ph.wait("tensor", "evA", ctx.val("evA"))
                            for mh in range(2):
                                for h2 in range(H):
                                    ph.run("tensor", lambda e, h2=h2, q=q, mh=mh: e.matmul(
                                        psS_[:, mh * TB : (mh + 1) * TB],
                                        Ocomb[:, h2 * TB + q * 128 : h2 * TB + q * 128 + 128],
                                        woT_sb[:, h2 * C + mh * 512 : h2 * C + (mh + 1) * 512],
                                        start=(h2 == 0), stop=(h2 == H - 1)),
                                        sem="mmA2" if h2 == H - 1 else None)
                            ph.wait("scalar", "mmA2", ctx.val("mmA2"))
                            tt = qb * (TB // 128) + q
                            for mh in range(2):
                                ph.run("scalar", lambda e, tt=tt, mh=mh: e.copy(
                                    zqA[:, tt * C + mh * 512 : tt * C + (mh + 1) * 512],
                                    psS_[:, mh * TB : (mh + 1) * TB]),
                                    sem="evA" if mh == 1 else None)
                        ph.wait("vector", "evA", ctx.val("evA"))
                    ph.wait("sync", "evA", ctx.val("evA"))
                    ph.build()
                qkv_st.close()

                ph = Phase(ctx, f"f_{s}")
                with contextlib.ExitStack() as pst:
                    Ep = pst.enter_context
                    xnb = Ep(nc.sbuf_tensor(f"xnb{s}", [128, NCH * TB], BF16))
                    dump = Ep(nc.sbuf_tensor(f"dumpF{s}", [128, 1024], BF16))
                    xnm = Ep(nc.sbuf_tensor(f"xnm{s}", [128, E * NCH * TB], BF16))
                    intr = Ep(nc.sbuf_tensor(f"intr{s}", [128, NIC * TB], BF16))
                    w1b = Ep(nc.sbuf_tensor(f"w1b{s}", [128, 2 * E * C], BF16))
                    wob = Ep(nc.sbuf_tensor(f"wob{s}", [128, 2 * C], BF16))
                    mskb = Ep(nc.sbuf_tensor(f"mskb{s}", [128, E * TB], BF16))
                    xfm2 = Ep(nc.sbuf_tensor(f"xfm2{s}", [128, 2 * TB], F32))
                    xtk2 = Ep(nc.sbuf_tensor(f"xtk2{s}", [128, C], F32))
                    xntk = Ep(nc.sbuf_tensor(f"xntk{s}", [128, C], F32))
                    ysb = Ep(nc.sbuf_tensor(f"ysb{s}", [128, C], F32))
                    fout = Ep(nc.sbuf_tensor(f"fout{s}", [128, C], F32))
                    b1_bf = Ep(nc.sbuf_tensor(f"b1bf{s}", [4, I], BF16))
                    grows4 = Ep(nc.sbuf_tensor(f"grows4{s}", [4, C], F32))
                    gam_b = Ep(nc.sbuf_tensor(f"gamb{s}", [128, C], F32))
                    bet_b = Ep(nc.sbuf_tensor(f"betb{s}", [128, C], F32))
                    bo_b = Ep(nc.sbuf_tensor(f"bob{s}", [128, C], F32))
                    psB2 = Ep(nc.psum_tensor(f"psB2{s}", [128, 2 * TB], F32))
                    psZ = Ep(nc.psum_tensor(f"psZ{s}", [128, 2 * TB], F32))
                    psY = Ep(nc.psum_tensor(f"psY{s}", [128, 2048], F32))
                    ph.dma("gpsimd", b1_bf[0:4, :], b1, "ldg")
                    ph.dma("sync", grows4[0:1, :], gam.rearrange("(a c) -> a c", a=1), "ldF")
                    ph.dma("sync", grows4[1:2, :], bet.rearrange("(a c) -> a c", a=1), "ldF")
                    ph.dma("sync", grows4[2:3, :], bo.rearrange("(a c) -> a c", a=1), "ldF")
                    ph.wait("tensor", "ldF", ctx.val("ldF"))
                    for j, dst in enumerate((gam_b, bet_b, bo_b)):
                        for mh in range(2):
                            ph.run("tensor", lambda e, j=j, mh=mh, psB2=psB2: e.matmul(
                                psB2[:, mh * TB : (mh + 1) * TB], sel4f[0:4, j * 128 : (j + 1) * 128],
                                grows4[0:4, mh * 512 : (mh + 1) * 512],
                                start=True, stop=True), sem="mmG")
                            ph.wait("scalar", "mmG", ctx.val("mmG"))
                            ph.run("scalar", lambda e, dst=dst, mh=mh, psB2=psB2: e.copy(
                                dst[:, mh * 512 : (mh + 1) * 512],
                                psB2[:, mh * TB : (mh + 1) * TB]), sem="evG")
                            ph.wait("tensor", "evG", ctx.val("evG"))
                    ph.wait("tensor", "ldg", ctx.val("ldg"))
                    stage_gxm = {}

                    def emit_stage(tb):
                        t0 = tb * TB
                        gtb = s * T + t0
                        if tb > 0:
                            ph.wait("tensor", "vxn", ctx.val("vxn"))
                        ph.run("tensor", lambda e, gtb=gtb, psB2=psB2: e.matmul(
                            psB2[:, 0:TB], sel4f[0:4, 0:128], srows4[0:4, gtb : gtb + TB],
                            start=True, stop=True))
                        ph.run("tensor", lambda e, gtb=gtb, psB2=psB2: e.matmul(
                            psB2[:, TB : 2 * TB], sel4f[0:4, 128:256],
                            srows4[0:4, gtb : gtb + TB], start=True, stop=True), sem="mmB2")
                        if tb > 0:
                            ph.wait("tensor", "sZ", ctx.val("sZ"))
                            ph.wait("vector", "gxm", ctx.val("gxm"))
                            ph.wait("vector", "mmZ", ctx.val("mmZ"))
                            ph.wait("gpsimd", "mmZ", ctx.val("mmZ"))
                            ph.wait("scalar", "gxm", ctx.val("gxm"))
                        for e_ in range(E):
                            pzh = psZ[:, (e_ % 2) * TB : (e_ % 2 + 1) * TB]
                            if e_ >= 2:
                                ph.wait("tensor", "evM", ctx.val("evM") - 1)
                            ph.run("tensor", lambda e, e_=e_, gtb=gtb, pzh=pzh: e.matmul(
                                pzh, sel4b[0:4, e_ * 128 : (e_ + 1) * 128],
                                masksr[0:4, gtb : gtb + TB],
                                start=True, stop=True), sem="mmM2")
                            ph.wait("scalar", "mmM2", ctx.val("mmM2"))
                            ph.run("scalar", lambda e, e_=e_, pzh=pzh: e.copy(
                                mskb[:, e_ * TB : (e_ + 1) * TB], pzh), sem="evM")
                        vldm = ctx.val("evM")
                        ph.wait("tensor", "evM", vldm)
                        ph.wait("vector", "mmB2", ctx.val("mmB2"))
                        ph.wait("vector", "evM", vldm)
                        ph.wait("gpsimd", "evM", vldm)
                        for a in range(NCH):
                            par = a % 2
                            if a >= 2:
                                ph.wait("sync", "vxn", ctx.val("vxn") - 1)
                            ph.dma("sync", xfm2[:, par * TB : (par + 1) * TB],
                                   x_fm[s, a * 128 : (a + 1) * 128, t0 : t0 + TB], "ldX")
                            ph.wait("vector", "ldX", ctx.val("ldX"))
                            ph.run("vector", lambda e, par=par, psB2=psB2: e.tensor_mul(
                                xfm2[:, par * TB : (par + 1) * TB],
                                xfm2[:, par * TB : (par + 1) * TB], psB2[:, 0:TB]))
                            ph.run("vector", lambda e, par=par, a=a, psB2=psB2: e.tensor_sub(
                                xnb[:, a * TB : (a + 1) * TB],
                                xfm2[:, par * TB : (par + 1) * TB], psB2[:, TB : 2 * TB]),
                                sem="vxn")
                            ph.wait("gpsimd", "vxn", ctx.val("vxn"))
                            for e_ in range(2):
                                ph.run("gpsimd", lambda e, e_=e_, a=a: e.tensor_mul(
                                    xnm[:, (e_ * NCH + a) * TB : (e_ * NCH + a + 1) * TB],
                                    xnb[:, a * TB : (a + 1) * TB],
                                    mskb[:, e_ * TB : (e_ + 1) * TB]),
                                    sem="gxm" if e_ == 1 else None)
                            for e_ in range(2, 4):
                                ph.run("vector", lambda e, e_=e_, a=a: e.tensor_mul(
                                    xnm[:, (e_ * NCH + a) * TB : (e_ * NCH + a + 1) * TB],
                                    xnb[:, a * TB : (a + 1) * TB],
                                    mskb[:, e_ * TB : (e_ + 1) * TB]),
                                    sem="gxm" if e_ == 3 else None)
                        stage_gxm[tb] = ctx.val("gxm")

                    def emit_zsel(tb):
                        t0 = tb * TB
                        gtb = s * T + t0
                        for ic in range(NIC):
                            par = ic % 2
                            wsl = w1b[:, par * E * C : (par + 1) * E * C]
                            for qi, qeng, sname in ((0, "sync", "ldW1"), (1, "scalar", "ldW1b")):
                                if ic >= 2:
                                    ph.wait(qeng, "mmZ", ctx.val("mmZ") - 1)
                                ph.dma(qeng,
                                       wsl[:, qi * 2 * C : (qi + 1) * 2 * C].rearrange(
                                           "p (e a i) -> p e a i", e=2, a=NCH),
                                       w1bf[qi * 2 : qi * 2 + 2, :, ic * 128 : (ic + 1) * 128]
                                       .rearrange("e (a p) i -> p e a i", p=128), sname)
                            if ic == 0:
                                ph.wait("tensor", "gxm", stage_gxm[tb])
                            ph.wait("tensor", "ldW1", ctx.val("ldW1"))
                            ph.wait("tensor", "ldW1b", ctx.val("ldW1b"))
                            if ic >= 2:
                                ph.wait("tensor", "sZ", ctx.val("sZ") - 1)
                            pz = psZ[:, par * TB : (par + 1) * TB]
                            first = True
                            for e_ in range(E):
                                for a in range(NCH):
                                    ph.run("tensor", lambda e, pz=pz, wsl=wsl, e_=e_, a=a,
                                           first=first: e.matmul(
                                        pz, wsl[:, e_ * C + a * 128 : e_ * C + a * 128 + 128],
                                        xnm[:, (e_ * NCH + a) * TB : (e_ * NCH + a + 1) * TB],
                                        start=first, stop=False))
                                    first = False
                            ph.run("tensor", lambda e, pz=pz, ic=ic, gtb=gtb: e.matmul(
                                pz, b1_bf[0:4, ic * 128 : (ic + 1) * 128],
                                masksr[0:4, gtb : gtb + TB], start=False, stop=True), sem="mmZ")
                            ph.wait("scalar", "mmZ", ctx.val("mmZ"))
                            ph.run("scalar", lambda e, pz=pz, ic=ic: e.activation(
                                intr[:, ic * TB : (ic + 1) * TB], pz, AF.Gelu), sem="sZ")

                    def emit_wo_epi(tb):
                        t0 = tb * TB
                        vsz = ctx.val("sZ")
                        ph.wait("tensor", "sZ", vsz)
                        for half in range(2):
                            for ic in range(NIC):
                                par = ic % 2
                                if ic >= 2:
                                    ph.wait("gpsimd", "mmYc", ctx.val("mmYc") - 1)
                                elif half == 1 or tb > 0:
                                    ph.wait("gpsimd", "mmYc", ctx.val("mmYc"))
                                ph.dma("gpsimd", wob[:, par * C : (par + 1) * C],
                                       wobf[ic * 128 : (ic + 1) * 128, :], "ldWo")
                                ph.wait("tensor", "ldWo", ctx.val("ldWo"))
                                if ic == 0 and (half == 1 or tb > 0):
                                    ph.wait("tensor", "syf", ctx.val("syf"))
                                for q in range(2):
                                    tt_l = half * 2 + q
                                    for mh in range(2):
                                        py = psY[:, (q * 2 + mh) * 512 : (q * 2 + mh + 1) * 512]
                                        ph.run("tensor", lambda e, py=py, ic=ic, tt_l=tt_l,
                                               mh=mh, par=par: e.matmul(
                                            py, intr[:, ic * TB + tt_l * 128 : ic * TB + tt_l * 128 + 128],
                                            wob[:, par * C + mh * 512 : par * C + (mh + 1) * 512],
                                            start=(ic == 0), stop=(ic == NIC - 1)),
                                            sem="mmYc" if (q == 1 and mh == 1) else None)
                            vmy = ctx.val("mmYc")
                            ph.wait("scalar", "mmYc", vmy)
                            ph.wait("vector", "mmYc", vmy)
                            for q in range(2):
                                tt_l = half * 2 + q
                                tt = t0 // 128 + tt_l
                                k = s * NT + tt
                                xk = xtk2[:, 0:C]
                                ph.wait("scalar", "vout", ctx.val("vout"))
                                ph.wait("scalar", "stO", ctx.val("stO"))
                                ph.dma("scalar", xk, x_tok[s, tt * 128 : (tt + 1) * 128, :],
                                       "ldXT")
                                for mh in range(2):
                                    py = psY[:, (q * 2 + mh) * 512 : (q * 2 + mh + 1) * 512]
                                    ph.run("vector", lambda e, py=py, mh=mh: e.tensor_add(
                                        ysb[:, mh * 512 : (mh + 1) * 512], py,
                                        bo_b[:, mh * 512 : (mh + 1) * 512]),
                                        sem="syf" if mh == 1 else None)
                                ph.wait("vector", "ldXT", ctx.val("ldXT"))
                                ph.run("vector", lambda e, xk=xk, k=k: e.tensor_scalar(
                                    xntk[:, :], xk, stats[:, k : k + 1],
                                    stats[:, 128 + k : 129 + k], ALU.mult, ALU.add))
                                for mh in range(2):
                                    ph.run("vector", lambda e, mh=mh: e.tensor_add(
                                        ysb[:, mh * 512 : (mh + 1) * 512],
                                        ysb[:, mh * 512 : (mh + 1) * 512],
                                        xntk[:, mh * 512 : (mh + 1) * 512]))
                                ph.rund("vector", lambda e: e.reduce_sum(sc[:, 0:1], ysb[:, :],
                                                                         axis=AX.X))
                                ph.rund("vector", lambda e: e.scalar_tensor_tensor(
                                    dump[:, 0:C], ysb[:, :], 1.0, ysb[:, :], ALU.mult, ALU.mult,
                                    accum_out=sc[:, 1:2]))
                                ph.rund("vector", lambda e: e.tensor_scalar_mul(
                                    sc[:, 0:1], sc[:, 0:1], 1.0 / C))
                                ph.rund("vector", lambda e: e.tensor_scalar_mul(
                                    sc[:, 1:2], sc[:, 1:2], 1.0 / C))
                                ph.rund("vector", lambda e: e.tensor_mul(sc[:, 2:3], sc[:, 0:1],
                                                                         sc[:, 0:1]))
                                ph.rund("vector", lambda e: e.tensor_sub(sc[:, 2:3], sc[:, 1:2],
                                                                         sc[:, 2:3]), sem="vy")
                                ph.wait("scalar", "vy", ctx.val("vy"))
                                ph.rund("scalar", lambda e: e.activation(
                                    sc[:, 5:6], sc[:, 2:3], AF.Sqrt, bias=1e-12), sem="sy0")
                                ph.wait("vector", "sy0", ctx.val("sy0"))
                                ph.rund("vector", lambda e: e.reciprocal(sc[:, 3:4], sc[:, 5:6]))
                                ph.rund("vector", lambda e: e.tensor_mul(sc[:, 4:5], sc[:, 0:1],
                                                                         sc[:, 3:4]))
                                ph.rund("vector", lambda e: e.tensor_scalar_mul(
                                    sc[:, 4:5], sc[:, 4:5], -1.0))
                                ph.run("vector", lambda e: e.tensor_scalar(
                                    fout[:, :], ysb[:, :], sc[:, 3:4], sc[:, 4:5],
                                    ALU.mult, ALU.add))
                                ph.run("vector", lambda e: e.tensor_mul(fout[:, :], fout[:, :],
                                                                        gam_b[:, :]))
                                ph.run("vector", lambda e: e.tensor_add(fout[:, :], fout[:, :],
                                                                        bet_b[:, :]))
                                ph.run("vector", lambda e, xk=xk: e.tensor_add(
                                    fout[:, :], fout[:, :], xk))
                                ph.run("vector", lambda e: e.tensor_add(fout[:, :], fout[:, :],
                                                                        xntk[:, :]))
                                ph.run("vector", lambda e, tt=tt: e.tensor_add(
                                    fout[:, :], fout[:, :], zqA[:, tt * C : (tt + 1) * C]),
                                    sem="vout")
                                ph.wait("sync", "vout", ctx.val("vout"))
                                ph.dma("sync", out[s, tt * 128 : (tt + 1) * 128, :],
                                       fout[:, :], "stO")

                    emit_stage(0)
                    for tb in range(NTB):
                        emit_zsel(tb)
                        if tb + 1 < NTB:
                            emit_stage(tb + 1)
                        emit_wo_epi(tb)
                    ph.wait("sync", "stO", ctx.val("stO"))
                    ph.build()

        if __debug__ and globals().get("ENDDUMP"):
            ph = Phase(ctx, "enddump")
            for nm, ap_, shp, dt_ in (("stats", stats[:, :], [128, 224], F32),
                                      ("srows4", srows4[0:4, :], [4, STW], F32),
                                      ("masksr", masksr[0:4, :], [4, SPC * T], BF16),
                                      ("kp", kp_all[:, :], [128, H * NT], F32)):
                t = nc.dram_tensor("dbg_" + nm, shp, dt_, kind="ExternalOutput").ap()
                ph.dma("sync", t, ap_, "dbgE")
            ph.wait("sync", "dbgE", ctx.val("dbgE"))
            ph.build()
    return nc


def _host_constants():
    inv = 1.0 / (10000.0 ** (np.arange(0, D, 2, dtype=np.float64) / D))
    t = np.arange(T, dtype=np.float64)
    fr = np.outer(inv, t)
    cos = np.concatenate([np.cos(fr), np.cos(fr)], axis=0)
    sinm = np.concatenate([np.sin(fr), np.sin(fr)], axis=0)
    rot = np.zeros((128, 128), np.float32)
    rot[:64, 64:] = -np.eye(64, dtype=np.float32)
    rot[64:, :64] = np.eye(64, dtype=np.float32)
    tri = (np.arange(128)[:, None] <= np.arange(128)[None, :]).astype(ml_dtypes.bfloat16)
    ident = np.eye(128)
    sel4 = np.zeros((4, 512), np.float32)
    for j in range(4):
        sel4[j, j * 128 : (j + 1) * 128] = 1.0
    return (cos.astype(np.float32), sinm.astype(np.float32), tri,
            ident.astype(ml_dtypes.bfloat16), ident.astype(np.float32),
            np.ones((1, 128), np.float32), sel4,
            np.ascontiguousarray(rot.T).astype(ml_dtypes.bfloat16))


def kernel(L0_attn_x, L0_attn_Wq, L0_attn_Wk, L0_attn_Wv, L0_attn_Wo,
           L0_attn_ln_g, L0_attn_ln_b, L0_attn_head_scales, L0_attn_head_dirs,
           L0_ffn_W_router, L0_ffn_W1, L0_ffn_b1, L0_ffn_Wo, L0_ffn_bo,
           L0_ffn_ln_gamma, L0_ffn_ln_beta):
    x = np.asarray(L0_attn_x, np.float32)
    asc = np.ascontiguousarray
    cos, sinm, tri, id_b, id_f, ones, sel4, rswap = _host_constants()
    shared = {
        "wqT": asc(np.asarray(L0_attn_Wq, np.float32).T),
        "wkT": asc(np.asarray(L0_attn_Wk, np.float32).T),
        "wvT": asc(np.asarray(L0_attn_Wv, np.float32).T),
        "woT": asc(np.asarray(L0_attn_Wo, np.float32).T),
        "ln_g": np.asarray(L0_attn_ln_g, np.float32),
        "ln_b": np.asarray(L0_attn_ln_b, np.float32),
        "hscale": np.asarray(L0_attn_head_scales, np.float32),
        "dirsT": asc(np.asarray(L0_attn_head_dirs, np.float32).T),
        "wr": np.asarray(L0_ffn_W_router, np.float32),
        "w1": np.asarray(L0_ffn_W1, np.float32),
        "b1": np.asarray(L0_ffn_b1, np.float32),
        "wo2": np.asarray(L0_ffn_Wo, np.float32),
        "bo": np.asarray(L0_ffn_bo, np.float32),
        "gam": np.asarray(L0_ffn_ln_gamma, np.float32),
        "bet": np.asarray(L0_ffn_ln_beta, np.float32),
        "ropec": cos, "ropes": sinm, "tri_c": tri,
        "ident_b": id_b, "ident_f": id_f, "ones_in": ones,
        "sel4f_in": sel4, "sel4b_in": sel4.astype(ml_dtypes.bfloat16),
        "rswapT": rswap,
    }
    if "nc" not in _CACHE:
        _CACHE["nc"] = build_program()
    nc = _CACHE["nc"]
    in_maps = []
    for i in range(NCORE):
        xs = x[i * SPC : (i + 1) * SPC]
        m = dict(shared)
        m["x_tok"] = asc(xs)
        m["x_fm"] = asc(xs.transpose(0, 2, 1))
        in_maps.append(m)
    res = run_bass_kernel_spmd(nc, in_maps, core_ids=list(range(NCORE)))
    return np.concatenate([r["out"] for r in res.results], axis=0)


# version 23
# speedup vs baseline: 1.2842x; 1.2842x over previous
import contextlib

import numpy as np
import ml_dtypes

import concourse.bass as bass
import concourse.mybir as mybir
from concourse.bass_utils import run_bass_kernel_spmd

F32 = mybir.dt.float32
BF16 = mybir.dt.bfloat16
AF = mybir.ActivationFunctionType
ALU = mybir.AluOpType
AX = mybir.AxisListType

B, T, C, H, D = 16, 2048, 1024, 8, 128
I, E = 4096, 4
NCORE = 8
SPC = B // NCORE
NT = T // 128
NCH = C // 128
NIC = I // 128
TB = 512
STW = SPC * T
NTB = T // TB
INV_SQRT_D = float(np.float32(1.0) / np.float32(np.sqrt(np.float64(D))))
SHIFT_MAIN = -4.0
SHIFT_GEO = -12.0

_CACHE = {}


class Ctx:

    def __init__(self, nc, stack):
        self.nc = nc
        self.stack = stack
        self.sems = {}
        self.cnt = {}

    def sem(self, name):
        if name not in self.sems:
            self.sems[name] = self.stack.enter_context(self.nc.semaphore(name))
            self.cnt[name] = 0
        return self.sems[name]

    def inc(self, name, k=1):
        self.sem(name)
        self.cnt[name] += k
        return self.cnt[name]

    def val(self, name):
        self.sem(name)
        return self.cnt[name]


class Phase:

    ENGS = ("sync", "gpsimd", "vector", "scalar", "tensor")

    def __init__(self, ctx, name):
        self.ctx = ctx
        self.name = name
        self.q = {e: [] for e in self.ENGS}

    def on(self, eng, fn):
        self.q[eng].append(fn)

    def dma(self, eng, out, in_, sem, slow=False):
        v = self.ctx.inc(sem, 16)
        s = self.ctx.sems[sem]
        self.on(eng, lambda e, out=out, in_=in_, s=s: e.dma_start(
            out=out, in_=in_, allow_slow_non_contiguous=slow).then_inc(s, 16))
        return v

    def wait(self, eng, sem, val):
        s = self.ctx.sem(sem)
        self.on(eng, lambda e, s=s, val=val: e.wait_ge(s, val))

    def run(self, eng, fn, sem=None):
        if sem is None:
            self.on(eng, fn)
            return None
        v = self.ctx.inc(sem, 1)
        s = self.ctx.sems[sem]
        self.on(eng, lambda e, fn=fn, s=s: fn(e).then_inc(s, 1))
        return v

    def rund(self, eng, fn, sem=None):
        self.on(eng, fn)
        if sem is None:
            self.on(eng, lambda e: e.drain())
            return None
        v = self.ctx.inc(sem, 1)
        s = self.ctx.sems[sem]
        self.on(eng, lambda e, s=s: e.drain().then_inc(s, 1))
        return v

    def build(self):
        nc = self.ctx.nc
        with nc.Block(self.name) as blk:
            for eng in self.ENGS:
                thunks = self.q[eng]
                if not thunks:
                    continue
                getattr(blk, eng)(lambda e, thunks=thunks: [f(e) for f in thunks])


def build_program():
    nc = bass.Bass("TRN2", target_bir_lowering=False, detect_race_conditions=False)
    dt = nc.dram_tensor

    x_tok = dt("x_tok", [SPC, T, C], F32, kind="ExternalInput").ap()
    x_fm = dt("x_fm", [SPC, C, T], F32, kind="ExternalInput").ap()
    wqT = dt("wqT", [C, C], F32, kind="ExternalInput").ap()
    wkT = dt("wkT", [C, C], F32, kind="ExternalInput").ap()
    wvT = dt("wvT", [C, C], F32, kind="ExternalInput").ap()
    woT = dt("woT", [C, C], F32, kind="ExternalInput").ap()
    ln_g = dt("ln_g", [C], F32, kind="ExternalInput").ap()
    ln_b = dt("ln_b", [C], F32, kind="ExternalInput").ap()
    hscale = dt("hscale", [H], F32, kind="ExternalInput").ap()
    dirsT = dt("dirsT", [3, H], F32, kind="ExternalInput").ap()
    wr = dt("wr", [C, E], F32, kind="ExternalInput").ap()
    w1 = dt("w1", [E, C, I], F32, kind="ExternalInput").ap()
    b1 = dt("b1", [E, I], F32, kind="ExternalInput").ap()
    wo2 = dt("wo2", [I, C], F32, kind="ExternalInput").ap()
    bo = dt("bo", [C], F32, kind="ExternalInput").ap()
    gam = dt("gam", [C], F32, kind="ExternalInput").ap()
    bet = dt("bet", [C], F32, kind="ExternalInput").ap()
    ropec = dt("ropec", [D, T], F32, kind="ExternalInput").ap()
    ropes = dt("ropes", [D, T], F32, kind="ExternalInput").ap()
    rswapT = dt("rswapT", [128, 128], BF16, kind="ExternalInput").ap()
    tri_in = dt("tri_c", [128, 128], BF16, kind="ExternalInput").ap()
    ident_b = dt("ident_b", [128, 128], BF16, kind="ExternalInput").ap()
    ident_f = dt("ident_f", [128, 128], F32, kind="ExternalInput").ap()
    ones_in = dt("ones_in", [1, 128], F32, kind="ExternalInput").ap()
    sel4f_in = dt("sel4f_in", [4, 4 * 128], F32, kind="ExternalInput").ap()
    sel4b_in = dt("sel4b_in", [4, 4 * 128], BF16, kind="ExternalInput").ap()

    out = dt("out", [SPC, T, C], F32, kind="ExternalOutput").ap()

    w1bf = dt("w1bf", [E, C, I], BF16, kind="Internal").ap()
    w1r = dt("w1r", [NIC, 128, E * NCH * 128], BF16, kind="Internal").ap()
    wobf = dt("wobf", [I, C], BF16, kind="Internal").ap()

    with contextlib.ExitStack() as st:
        En = st.enter_context
        ctx = Ctx(nc, st)

        stats = En(nc.sbuf_tensor("stats", [128, 224], F32))
        srows4 = En(nc.sbuf_tensor("srows4", [4, STW], F32))
        masksr = En(nc.sbuf_tensor("masksr", [4, SPC * T], BF16))
        sel4f = En(nc.sbuf_tensor("sel4f", [4, 4 * 128], F32))
        sel4b = En(nc.sbuf_tensor("sel4b", [4, 4 * 128], BF16))
        g_col = En(nc.sbuf_tensor("g_col", [128, NCH], F32))
        b_col = En(nc.sbuf_tensor("b_col", [128, NCH], F32))
        hs_r = En(nc.sbuf_tensor("hs_r", [1, H], F32))
        omh_r = En(nc.sbuf_tensor("omh_r", [1, H], F32))
        dirs_bf = En(nc.sbuf_tensor("dirs_bf", [3, H], BF16))
        dirs_rep = En(nc.sbuf_tensor("dirs_rep", [3, H * 128], BF16))
        tri = En(nc.sbuf_tensor("tri", [128, 128], BF16))
        rsw = En(nc.sbuf_tensor("rsw", [128, 128], BF16))
        id_b = En(nc.sbuf_tensor("id_b", [128, 128], BF16))
        ones_cb = En(nc.sbuf_tensor("ones_cb", [128, 1], BF16))
        ones_cf = En(nc.sbuf_tensor("ones_cf", [128, 1], F32))
        ones_rf = En(nc.sbuf_tensor("ones_rf", [1, 128], F32))
        ones_rb = En(nc.sbuf_tensor("ones_rb", [1, 128], BF16))
        ones3f = En(nc.sbuf_tensor("ones3f", [3, 128], F32))
        dirsT_sb = En(nc.sbuf_tensor("dirsT_sb", [3, H], F32))
        wr_sb = En(nc.sbuf_tensor("wr_sb", [128, NCH * E], F32))
        sum_r = En(nc.sbuf_tensor("sum_r", [1, E], F32))
        sum_b = En(nc.sbuf_tensor("sum_b", [128, E], F32))
        consts = En(nc.sbuf_tensor("consts", [128, 8], F32))
        sc = En(nc.sbuf_tensor("sc", [128, 8], F32))
        lgt = En(nc.sbuf_tensor("lgt", [128, E], F32))
        oh_t = En(nc.sbuf_tensor("oh_t", [128, E], BF16))
        kp_all = En(nc.sbuf_tensor("kp_all", [128, H * NT], F32))
        rows_t = En(nc.sbuf_tensor("rows_t", [128, 128], F32))

        CONST_VALS = [INV_SQRT_D, SHIFT_MAIN, SHIFT_GEO, 1e-5, 1e-12]
        for i, v in enumerate(CONST_VALS):
            nc.const_aps.register(F32, v, consts[:, i : i + 1])

        ph = Phase(ctx, "init")
        for i, v in enumerate(CONST_VALS):
            ph.rund("vector", lambda e, i=i, v=v: e.memset(consts[:, i : i + 1], v))
        ph.rund("vector", lambda e: e.memset(ones_cb[:, :], 1.0))
        ph.rund("vector", lambda e: e.memset(ones_cf[:, :], 1.0), sem="vI")
        ph.dma("sync", g_col[:, :], ln_g.rearrange("(a p) -> p a", p=128), "ld0", slow=True)
        ph.dma("sync", b_col[:, :], ln_b.rearrange("(a p) -> p a", p=128), "ld0", slow=True)
        ph.dma("sync", hs_r[0:1, :], hscale.rearrange("(a h) -> a h", a=1), "ld0")
        ph.dma("sync", tri[:, :], tri_in, "ld0")
        ph.dma("sync", rsw[:, :], rswapT, "ld0")
        ph.dma("sync", id_b[:, :], ident_b, "ld0")
        ph.dma("sync", ones_rf[0:1, :], ones_in, "ld0")
        ph.dma("sync", wr_sb[:, :].rearrange("p (a e) -> p a e", a=NCH),
               wr.rearrange("(a p) e -> p a e", p=128), "ld0", slow=True)
        ph.dma("gpsimd", dirs_bf[0:3, :], dirsT, "ldg")
        ph.dma("sync", dirsT_sb[0:3, :], dirsT, "ld0")
        ph.dma("sync", sel4f[0:4, :], sel4f_in, "ld0")
        ph.dma("sync", sel4b[0:4, :], sel4b_in, "ld0")
        v_ld0 = ctx.val("ld0")
        v_ldg = ctx.val("ldg")
        ph.rund("vector", lambda e: e.memset(ones_rb[0:1, :], 1.0))
        ph.rund("vector", lambda e: e.memset(ones3f[0:3, :], 1.0))
        ph.wait("vector", "ld0", v_ld0)
        for h in range(H):
            ph.run("vector", lambda e, h=h: e.tensor_single_scalar(
                dirs_rep[0:3, h * 128 : (h + 1) * 128], ones3f[0:3, :],
                dirsT_sb[0:3, h : h + 1], ALU.mult))
        ph.rund("vector", lambda e: e.tensor_scalar(omh_r[0:1, :], hs_r[0:1, :], -1.0, 1.0,
                                                    ALU.mult, ALU.add))
        for q in range(4):
            sl = slice(q * 1024, (q + 1) * 1024)
            ph.dma("gpsimd", wobf[sl, :], wo2[sl, :], "ldg")
        ph.wait("tensor", "ld0", v_ld0)
        ph.wait("tensor", "vI", ctx.val("vI"))
        with nc.psum_tensor("psI", [1, E], F32) as psI, \
                nc.psum_tensor("psI2", [128, 1024], F32) as psI2:
            for a in range(NCH):
                ph.run("tensor", lambda e, a=a, psI=psI: e.matmul(
                    psI[0:1, :], ones_cf[:, :], wr_sb[:, a * E : (a + 1) * E],
                    start=(a == 0), stop=(a == NCH - 1)),
                    sem="mmI" if a == NCH - 1 else None)
            ph.wait("scalar", "mmI", ctx.val("mmI"))
            ph.run("scalar", lambda e, psI=psI: e.copy(sum_r[0:1, :], psI[0:1, :]), sem="evI")
            ph.wait("tensor", "evI", ctx.val("evI"))
            bt = [(sum_b[:, 0:E], sum_r[0:1, 0:E])]
            for bi, (dst, srcrow) in enumerate(bt):
                n = dst.shape[1]
                ph.run("tensor", lambda e, srcrow=srcrow, n=n, psI2=psI2: e.matmul(
                    psI2[:, 0:n], ones_rf[0:1, :], srcrow, start=True, stop=True), sem="mmI")
                ph.wait("scalar", "mmI", ctx.val("mmI"))
                ph.run("scalar", lambda e, dst=dst, n=n, psI2=psI2: e.copy(dst, psI2[:, 0:n]),
                       sem="evI")
                ph.wait("tensor", "evI", ctx.val("evI"))
            ph.wait("sync", "evI", ctx.val("evI"))
            ph.wait("sync", "ldg", ctx.val("ldg"))
            ph.build()

        ph = Phase(ctx, "statsA")
        with contextlib.ExitStack() as pst:
            Ep = pst.enter_context
            xt_buf = Ep(nc.sbuf_tensor("xt_buf", [128, 2 * C], F32))
            dump = Ep(nc.sbuf_tensor("dumpA", [128, 1024], BF16))
            psT = Ep(nc.psum_tensor("psT", [128, 128], F32))
            for s in range(SPC):
                for tt in range(NT):
                    k = s * NT + tt
                    par = k % 2
                    xt = xt_buf[:, par * C : (par + 1) * C]
                    if k >= 2:
                        ph.wait("sync", "stA", ctx.val("stA") - 1)
                    ph.dma("sync", xt, x_tok[s, tt * 128 : (tt + 1) * 128, :], f"ldA{par}")
                    ph.wait("vector", f"ldA{par}", ctx.val(f"ldA{par}"))
                    ph.rund("vector", lambda e, xt=xt: e.reduce_sum(sc[:, 0:1], xt, axis=AX.X))
                    ph.rund("vector", lambda e, xt=xt: e.scalar_tensor_tensor(
                        dump[:, :], xt, 1.0, xt, ALU.mult, ALU.mult, accum_out=sc[:, 1:2]))
                    mu = stats[:, 160 + k : 161 + k]
                    ph.rund("vector", lambda e, mu=mu: e.tensor_scalar_mul(mu, sc[:, 0:1], 1.0 / C))
                    ph.rund("vector", lambda e: e.tensor_scalar_mul(sc[:, 1:2], sc[:, 1:2], 1.0 / C))
                    ph.rund("vector", lambda e, mu=mu: e.tensor_mul(sc[:, 2:3], mu, mu))
                    ph.rund("vector", lambda e: e.tensor_sub(sc[:, 2:3], sc[:, 1:2], sc[:, 2:3]),
                            sem="vA")
                    ph.wait("scalar", "vA", ctx.val("vA"))
                    r1 = stats[:, k : k + 1]
                    ph.rund("scalar", lambda e: e.activation(sc[:, 5:6], sc[:, 2:3], AF.Sqrt,
                                                             bias=1e-5), sem="sA")
                    ph.wait("vector", "sA", ctx.val("sA"))
                    ph.rund("vector", lambda e, r1=r1: e.reciprocal(r1, sc[:, 5:6]))
                    ph.rund("vector", lambda e, r1=r1: e.tensor_mul(sc[:, 3:4], r1, r1))
                    ph.rund("vector", lambda e: e.tensor_mul(sc[:, 3:4], sc[:, 2:3], sc[:, 3:4]),
                            sem="vA")
                    ph.wait("scalar", "vA", ctx.val("vA"))
                    ph.rund("scalar", lambda e: e.activation(sc[:, 5:6], sc[:, 3:4], AF.Sqrt,
                                                             bias=1e-5), sem="sA")
                    ph.wait("vector", "sA", ctx.val("sA"))
                    ph.rund("vector", lambda e: e.reciprocal(sc[:, 4:5], sc[:, 5:6]))
                    sa = stats[:, 64 + k : 65 + k]
                    ph.rund("vector", lambda e, r1=r1, sa=sa: e.tensor_mul(sa, r1, sc[:, 4:5]))
                    ph.rund("vector", lambda e, r1=r1, mu=mu, k=k: e.tensor_mul(
                        stats[:, 32 + k : 33 + k], mu, r1))
                    ph.rund("vector", lambda e, sa=sa, mu=mu, k=k: e.tensor_mul(
                        stats[:, 96 + k : 97 + k], mu, sa))
                    ph.rund("vector", lambda e, k=k: e.tensor_scalar_mul(
                        stats[:, 128 + k : 129 + k], stats[:, 32 + k : 33 + k], -1.0), sem="stA")
            ph.wait("tensor", "stA", ctx.val("stA"))
            idf_sb = Ep(nc.sbuf_tensor("idf_sb", [128, 128], F32))
            ph.dma("sync", idf_sb[:, :], ident_f, "ldA")
            ph.wait("tensor", "ldA", ctx.val("ldA"))
            ph.run("tensor", lambda e: e.transpose(psT[:, :], stats[:, 0:128], idf_sb[:, :]),
                   sem="mmT")
            ph.wait("scalar", "mmT", ctx.val("mmT"))
            ph.run("scalar", lambda e: e.copy(rows_t[:, :], psT[:, :]), sem="sA")
            ph.wait("sync", "sA", ctx.val("sA"))
            for j in range(4):
                ph.dma("sync",
                       srows4[j : j + 1, :].rearrange("p (k f) -> p k f", k=32),
                       rows_t[j * 32 : (j + 1) * 32, 0:128], "ldA")
            ph.wait("sync", "ldA", ctx.val("ldA"))
            ph.build()

        for s in range(SPC):
            with contextlib.ExitStack() as sst:
                Es = sst.enter_context
                zqA = Es(nc.sbuf_tensor(f"zqA{s}", [128, NCH * T], BF16))

                ph = Phase(ctx, f"c1_{s}")
                with contextlib.ExitStack() as pst:
                    Ep = pst.enter_context
                    xfm = Ep(nc.sbuf_tensor(f"xfm{s}", [128, NCH * TB], F32))
                    tmpf = Ep(nc.sbuf_tensor(f"tmpf{s}", [128, TB], F32))
                    psB = Ep(nc.psum_tensor(f"psB{s}", [128, 2 * TB], F32))
                    psR = Ep(nc.psum_tensor(f"psR{s}", [128, E], F32))
                    psM = Ep(nc.psum_tensor(f"psM{s}", [4, 128], BF16))
                    for tb in range(NTB):
                        t0 = tb * TB
                        if tb > 0:
                            ph.wait("tensor", "zC", ctx.val("zC"))
                        ph.run("tensor", lambda e, t0=t0, psB=psB: e.matmul(
                            psB[:, 0:TB], sel4f[0:4, 2 * 128 : 3 * 128],
                            srows4[0:4, s * T + t0 : s * T + t0 + TB],
                            start=True, stop=True))
                        ph.run("tensor", lambda e, t0=t0, psB=psB: e.matmul(
                            psB[:, TB : 2 * TB], sel4f[0:4, 3 * 128 : 4 * 128],
                            srows4[0:4, s * T + t0 : s * T + t0 + TB],
                            start=True, stop=True), sem="mmC")
                        if tb > 0:
                            ph.wait("sync", "zC", ctx.val("zC"))
                            ph.wait("sync", "vR", ctx.val("vR"))
                        for a in range(NCH):
                            ph.dma("sync", xfm[:, a * TB : (a + 1) * TB],
                                   x_fm[s, a * 128 : (a + 1) * 128, t0 : t0 + TB], "ldC")
                        ph.wait("vector", "ldC", ctx.val("ldC"))
                        ph.wait("vector", "mmC", ctx.val("mmC"))
                        for a in range(NCH):
                            xa = xfm[:, a * TB : (a + 1) * TB]
                            ph.run("vector", lambda e, xa=xa, psB=psB: e.tensor_mul(
                                tmpf[:, :], xa, psB[:, 0:TB]))
                            ph.run("vector", lambda e, psB=psB: e.tensor_sub(
                                tmpf[:, :], tmpf[:, :], psB[:, TB : 2 * TB]))
                            ph.run("vector", lambda e, a=a, t0=t0: e.tensor_scalar(
                                zqA[:, a * T + t0 : a * T + t0 + TB], tmpf[:, :],
                                g_col[:, a : a + 1], b_col[:, a : a + 1], ALU.mult, ALU.add),
                                sem="zC" if a == NCH - 1 else None)
                        ph.wait("tensor", "ldC", ctx.val("ldC"))
                        for q in range(TB // 128):
                            tt = t0 // 128 + q
                            k = s * NT + tt
                            if tb + q > 0:
                                ph.wait("tensor", "vR", ctx.val("vR"))
                            for a in range(NCH):
                                ph.run("tensor", lambda e, a=a, q=q, psR=psR: e.matmul(
                                    psR[:, :], xfm[:, a * TB + q * 128 : a * TB + q * 128 + 128],
                                    wr_sb[:, a * E : (a + 1) * E],
                                    start=(a == 0), stop=(a == NCH - 1)),
                                    sem="mmR" if a == NCH - 1 else None)
                            ph.wait("vector", "mmR", ctx.val("mmR"))
                            if tb + q > 0:
                                ph.wait("vector", "sM", ctx.val("sM"))
                            ph.rund("vector", lambda e, k=k, psR=psR: e.tensor_single_scalar(
                                lgt[:, :], psR[:, :], stats[:, k : k + 1], ALU.mult))
                            ph.rund("vector", lambda e, k=k: e.scalar_tensor_tensor(
                                lgt[:, :], sum_b[:, :], stats[:, 128 + k : 129 + k], lgt[:, :],
                                ALU.mult, ALU.add))
                            ph.rund("vector", lambda e: e.reduce_max(sc[:, 6:7], lgt[:, :],
                                                                     axis=AX.X))
                            ph.rund("vector", lambda e: e.tensor_single_scalar(
                                oh_t[:, :], lgt[:, :], sc[:, 6:7], ALU.is_ge), sem="vR")
                            ph.wait("tensor", "vR", ctx.val("vR"))
                            if tb + q > 0:
                                ph.wait("tensor", "sM", ctx.val("sM"))
                            ph.run("tensor", lambda e, psM=psM: e.transpose(
                                psM[:, :], oh_t[:, :], id_b[:, :]), sem="mmM")
                            ph.wait("scalar", "mmM", ctx.val("mmM"))
                            ph.run("scalar", lambda e, tt=tt, psM=psM: e.copy(
                                masksr[0:4, s * T + tt * 128 : s * T + tt * 128 + 128],
                                psM[0:4, :]), sem="sM")
                    ph.wait("sync", "sM", ctx.val("sM"))
                    ph.wait("sync", "zC", ctx.val("zC"))
                    ph.build()

                qkv_st = contextlib.ExitStack()
                Qfm = qkv_st.enter_context(nc.sbuf_tensor(f"Qfm{s}", [128, H * T], BF16))
                Kfm = qkv_st.enter_context(nc.sbuf_tensor(f"Kfm{s}", [128, H * T], BF16))
                Vtk = qkv_st.enter_context(nc.sbuf_tensor(f"Vtk{s}", [128, NT * C], BF16))

                ph = Phase(ctx, f"c2_{s}")
                with contextlib.ExitStack() as pst:
                    Ep = pst.enter_context
                    wbuf = Ep(nc.sbuf_tensor(f"wbuf{s}", [128, NCH * C], BF16))
                    psP = Ep(nc.psum_tensor(f"psP{s}", [128, 4 * TB], F32))
                    rtmpq = Ep(nc.sbuf_tensor(f"rtmpq{s}", [128, T], BF16))
                    rtmpk = Ep(nc.sbuf_tensor(f"rtmpk{s}", [128, T], BF16))
                    ropec_bf = Ep(nc.sbuf_tensor(f"ropecb{s}", [D, T], BF16))
                    ropes_bf = Ep(nc.sbuf_tensor(f"ropesb{s}", [D, T], BF16))
                    ph.dma("gpsimd", ropec_bf[:, :], ropec, "ldg")
                    ph.dma("gpsimd", ropes_bf[:, :], ropes, "ldg")
                    grp = 0
                    evs = {}
                    for wi, (wsrc, dst) in enumerate(((wqT, Qfm), (wkT, Kfm), (wvT, Vtk))):
                        if wi > 0:
                            ph.wait("gpsimd", "mmP", ctx.val("mmP"))
                        ph.dma("gpsimd", wbuf[:, :].rearrange("p (a m) -> p a m", a=NCH),
                               wsrc.rearrange("(a p) m -> p a m", p=128), "ldW")
                        ph.wait("tensor", "ldW", ctx.val("ldW"))
                        specs = ([(mc, tb) for mc in range(NCH) for tb in range(NTB)] if wi < 2
                                 else [(tt, mh) for tt in range(NT) for mh in range(2)])
                        for u, v_ in specs:
                            ps = psP[:, (grp % 4) * TB : (grp % 4 + 1) * TB]
                            if grp >= 4:
                                ph.wait("tensor", "evP", evs[grp - 4])
                            for a in range(NCH):
                                if wi < 2:
                                    fn = lambda e, ps=ps, a=a, mc=u, tb=v_: e.matmul(
                                        ps, wbuf[:, a * C + mc * 128 : a * C + mc * 128 + 128],
                                        zqA[:, a * T + tb * TB : a * T + (tb + 1) * TB],
                                        start=(a == 0), stop=(a == NCH - 1))
                                else:
                                    fn = lambda e, ps=ps, a=a, tt=u, mh=v_: e.matmul(
                                        ps, zqA[:, a * T + tt * 128 : a * T + tt * 128 + 128],
                                        wbuf[:, a * C + mh * 512 : a * C + (mh + 1) * 512],
                                        start=(a == 0), stop=(a == NCH - 1))
                                ph.run("tensor", fn, sem="mmP" if a == NCH - 1 else None)
                            ph.wait("scalar", "mmP", ctx.val("mmP"))
                            if wi < 2:
                                dsl = dst[:, u * T + v_ * TB : u * T + (v_ + 1) * TB]
                            else:
                                dsl = Vtk[:, u * C + v_ * 512 : u * C + (v_ + 1) * 512]
                            ph.run("scalar", lambda e, ps=ps, dsl=dsl: e.copy(dsl, ps), sem="evP")
                            evs[grp] = ctx.val("evP")
                            grp += 1
                    vev = ctx.val("evP")
                    vldg = ctx.val("ldg")
                    ph.wait("tensor", "evP", vev)
                    ph.wait("scalar", "ldg", vldg)
                    ph.wait("vector", "ldg", vldg)
                    ph.wait("gpsimd", "ldg", vldg)
                    jobs = [(eng, dst, rtmp, h)
                            for h in range(H)
                            for eng, dst, rtmp in (("vector", Qfm, rtmpq),
                                                   ("gpsimd", Kfm, rtmpk))]
                    for ji, (eng, dst, rtmp, h) in enumerate(jobs):
                        q_ = dst[:, h * T : (h + 1) * T]
                        if ji > 0:
                            ph.wait("tensor", "sro", ctx.val("sro"))
                        for tq in range(4):
                            ph.run("tensor", lambda e, q_=q_, tq=tq: e.matmul(
                                psP[:, tq * TB : (tq + 1) * TB], rsw[:, :],
                                q_[:, tq * TB : (tq + 1) * TB], start=True, stop=True),
                                sem="mmRo" if tq == 3 else None)
                        ph.wait("scalar", "mmRo", ctx.val("mmRo"))
                        if ji >= 2:
                            ph.wait("scalar", "rp", ctx.val("rp") - 1)
                        ph.run("scalar", lambda e, rtmp=rtmp: e.copy(rtmp[:, :], psP[:, :]),
                               sem="sro")
                        ph.wait(eng, "sro", ctx.val("sro"))
                        ph.run(eng, lambda e, rtmp=rtmp: e.tensor_mul(
                            rtmp[:, :], rtmp[:, :], ropes_bf[:, :]))
                        ph.run(eng, lambda e, q_=q_: e.tensor_mul(q_, q_, ropec_bf[:, :]))
                        ph.run(eng, lambda e, q_=q_, rtmp=rtmp: e.tensor_add(
                            q_, q_, rtmp[:, :]), sem="rp")
                    ph.wait("sync", "rp", ctx.val("rp"))
                    ph.build()

                ph = Phase(ctx, f"d_{s}")
                with contextlib.ExitStack() as pst:
                    Ep = pst.enter_context
                    if s == 0:
                        for e_ in range(E):
                            for ic_ in range(NIC):
                                ph.dma("gpsimd",
                                       w1r[ic_, :, e_ * 1024 : (e_ + 1) * 1024].rearrange(
                                           "p (a i) -> p a i", a=NCH),
                                       w1[e_, :, ic_ * 128 : (ic_ + 1) * 128].rearrange(
                                           "(a p) i -> p a i", p=128), "ldPC")
                    Ocomb = Ep(nc.sbuf_tensor(f"Oc{s}", [128, H * TB], BF16))
                    woT_sb = Ep(nc.sbuf_tensor(f"woTs{s}", [128, NCH * C], BF16))
                    ph.dma("gpsimd", woT_sb[:, :].rearrange("p (a m) -> p a m", a=NCH),
                           woT.rearrange("(a p) m -> p a m", p=128), "ldg")
                    ph.wait("tensor", "ldg", ctx.val("ldg"))
                    Pm = Ep(nc.sbuf_tensor(f"Pm{s}", [128, 3 * TB], BF16))
                    Pg = Ep(nc.sbuf_tensor(f"Pg{s}", [128, 3 * TB], BF16))
                    qpsb = Ep(nc.sbuf_tensor(f"qpsb{s}", [128, TB], BF16))
                    mixf = Ep(nc.sbuf_tensor(f"mixf{s}", [128, TB], BF16))
                    lrow = Ep(nc.sbuf_tensor(f"lrow{s}", [33, 2 * TB], BF16))
                    abrow = Ep(nc.sbuf_tensor(f"abrow{s}", [1, 2 * TB], F32))
                    absb = Ep(nc.sbuf_tensor(f"absb{s}", [128, 2 * TB], F32))
                    OMsb = Ep(nc.sbuf_tensor(f"OMsb{s}", [128, 2 * TB], BF16))
                    OGsb = Ep(nc.sbuf_tensor(f"OGsb{s}", [128, 2 * TB], BF16))
                    with nc.psum_tensor(f"psKP{s}", [128, NT], F32) as psKP:
                        for h in range(H):
                            if h > 0:
                                ph.wait("tensor", "evKP", ctx.val("evKP"))
                            for kt in range(NT):
                                ph.run("tensor", lambda e, h=h, kt=kt, psKP=psKP: e.matmul(
                                    psKP[:, kt : kt + 1],
                                    Kfm[0:3, h * T + kt * 128 : h * T + kt * 128 + 128],
                                    dirs_bf[0:3, h : h + 1],
                                    start=(kt == 0), stop=(kt == NT - 1)),
                                    sem="mmKP" if kt == NT - 1 else None)
                            ph.wait("vector", "mmKP", ctx.val("mmKP"))
                            ph.rund("vector", lambda e, h=h, psKP=psKP: e.tensor_copy(
                                kp_all[:, h * NT : (h + 1) * NT], psKP[:, :]), sem="evKP")
                        ph.wait("tensor", "evKP", ctx.val("evKP"))

                    psS_ = Ep(nc.psum_tensor(f"psS{s}", [128, 3 * TB], F32))
                    psQP = Ep(nc.psum_tensor(f"psQP{s}", [128, TB], F32))
                    psOM = Ep(nc.psum_tensor(f"psOM{s}", [128, TB], F32))
                    psOG = Ep(nc.psum_tensor(f"psOG{s}", [128, TB], F32))
                    psLM = Ep(nc.psum_tensor(f"psLM{s}", [1, TB], F32))
                    psLG = Ep(nc.psum_tensor(f"psLG{s}", [1, TB], F32))


                    def emit_mix_tail(dd, stage):
                        hq, hp, ev = dd[0], dd[1], dd[2]
                        arow = abrow[0:1, :]
                        lr = lrow[hp * 32 : hp * 32 + 1, :]
                        ab = absb[:, :]
                        if stage == 0:
                            ph.wait("vector", "evO", ev)
                            ph.rund("vector", lambda e, arow=arow, lr=lr: e.reciprocal(
                                arow[:, 0:TB], lr[:, 0:TB]))
                            ph.rund("vector", lambda e, arow=arow, lr=lr: e.reciprocal(
                                arow[:, TB : 2 * TB], lr[:, TB : 2 * TB]))
                            ph.rund("vector", lambda e, arow=arow, hq=hq: e.tensor_single_scalar(
                                arow[:, 0:TB], arow[:, 0:TB], omh_r[0:1, hq : hq + 1], ALU.mult))
                            ph.rund("vector", lambda e, arow=arow, hq=hq: e.tensor_single_scalar(
                                arow[:, TB : 2 * TB], arow[:, TB : 2 * TB],
                                hs_r[0:1, hq : hq + 1], ALU.mult), sem="vab")
                            dd_v = ctx.val("vab")
                            return dd + (dd_v,)
                        if stage == 1:
                            ph.wait("tensor", "sQP", ctx.val("sQP"))
                            ph.wait("tensor", "vab", dd[3])
                            ph.run("tensor", lambda e, arow=arow: e.matmul(
                                psQP[:, :], ones_rf[0:1, :], arow[0:1, 0:TB],
                                start=True, stop=True), sem="mmABa")
                            return dd + (ctx.val("mmABa"),)
                        if stage == 2:
                            ph.wait("scalar", "voc", ctx.val("voc"))
                            ph.wait("scalar", "mmABa", dd[4])
                            ph.run("scalar", lambda e, ab=ab: e.copy(ab[:, 0:TB], psQP[:, :]),
                                   sem="sabA")
                            return dd + (ctx.val("sabA"),)
                        if stage == 3:
                            ph.wait("tensor", "sabA", dd[5])
                            ph.run("tensor", lambda e, arow=arow: e.matmul(
                                psQP[:, :], ones_rf[0:1, :], arow[0:1, TB : 2 * TB],
                                start=True, stop=True), sem="mmABb")
                            return dd + (ctx.val("mmABb"),)
                        if stage == 4:
                            ph.wait("scalar", "mmABb", dd[6])
                            ph.run("scalar", lambda e, ab=ab: e.copy(ab[:, TB : 2 * TB],
                                                                     psQP[:, :]), sem="sabB")
                            return dd + (ctx.val("sabB"),)
                        ph.wait("vector", "sabB", dd[7])
                        om = OMsb[:, hp * TB : (hp + 1) * TB]
                        og = OGsb[:, hp * TB : (hp + 1) * TB]
                        oc = Ocomb[:, hq * TB : (hq + 1) * TB]
                        ph.run("vector", lambda e, om=om, ab=ab: e.tensor_mul(
                            mixf[:, :], om, ab[:, 0:TB]))
                        ph.run("vector", lambda e, og=og, ab=ab, oc=oc: e.tensor_mul(
                            oc, og, ab[:, TB : 2 * TB]))
                        ph.run("vector", lambda e, oc=oc: e.tensor_add(
                            oc, oc, mixf[:, :]), sem="voc")
                        return None

                    deferred = None
                    for qb in range(NTB):
                        nkt = 4 * qb + 4
                        for h in range(H):
                            hp = h % 2
                            if h == 0 and qb > 0:
                                ph.wait("tensor", "evA", ctx.val("evA"))
                            if deferred is not None:
                                deferred = emit_mix_tail(deferred, 0)
                            ph.wait("tensor", "sabB", ctx.val("sabB"))
                            ph.wait("tensor", "sm", ctx.val("sm"))
                            ph.run("tensor", lambda e, h=h, qb=qb: e.matmul(
                                psQP[:, :], dirs_rep[0:3, h * 128 : (h + 1) * 128],
                                Qfm[0:3, h * T + qb * TB : h * T + (qb + 1) * TB],
                                start=True, stop=True), sem="mmQP")
                            ph.wait("scalar", "mmQP", ctx.val("mmQP"))
                            ph.wait("scalar", "sg", ctx.val("sg"))
                            ph.run("scalar", lambda e: e.copy(qpsb[:, :], psQP[:, :]), sem="sQP")
                            prev_pv = ctx.val("mmPV")
                            prev_evo = ctx.val("evO")
                            sm_done, sg_done, pv_done, vd_done = {}, {}, {}, {}

                            def emit_S(kt):
                                par = kt % 3
                                pss = psS_[:, par * TB : (par + 1) * TB]
                                pmm = Pm[:, par * TB : (par + 1) * TB]
                                pgg = Pg[:, par * TB : (par + 1) * TB]
                                diag = kt >= 4 * qb
                                dlt = (kt - 4 * qb) * 128 if diag else 0
                                if kt >= 3:
                                    ph.wait("tensor", "sm", sm_done[kt - 3])
                                ph.run("tensor", lambda e, pss=pss, h=h, qb=qb, kt=kt: e.matmul(
                                    pss, Kfm[:, h * T + kt * 128 : h * T + kt * 128 + 128],
                                    Qfm[:, h * T + qb * TB : h * T + (qb + 1) * TB],
                                    start=True, stop=True), sem="mmS")
                                ph.wait("scalar", "mmS", ctx.val("mmS"))
                                ph.wait("scalar", "mmPV",
                                        pv_done[kt - 3] if kt >= 3 else prev_pv)
                                ph.run("scalar", lambda e, pmm=pmm, pss=pss, dlt=dlt: e.activation(
                                    pmm[:, dlt:TB], pss[:, dlt:TB], AF.Exp,
                                    bias=SHIFT_MAIN, scale=INV_SQRT_D), sem="sm")
                                sm_done[kt] = ctx.val("sm")
                                kps = kp_all[:, h * NT + kt : h * NT + kt + 1]
                                ph.run("scalar", lambda e, pgg=pgg, dlt=dlt, kps=kps: e.activation(
                                    pgg[:, dlt:TB], qpsb[:, dlt:TB], AF.Exp,
                                    bias=SHIFT_GEO, scale=kps), sem="sg")
                                sg_done[kt] = ctx.val("sg")
                                if diag:
                                    ph.wait("vector", "sm", sm_done[kt])
                                    ph.wait("vector", "sg", sg_done[kt])
                                    for pp in (pmm, pgg):
                                        if dlt > 0:
                                            ph.run("vector", lambda e, pp=pp, dlt=dlt: e.memset(
                                                pp[:, 0:dlt], 0.0))
                                        ph.run("vector", lambda e, pp=pp, dlt=dlt: e.tensor_mul(
                                            pp[:, dlt : dlt + 128], pp[:, dlt : dlt + 128],
                                            tri[:, :]), sem="vd")
                                    vd_done[kt] = ctx.val("vd")

                            def emit_PV(kt):
                                par = kt % 3
                                pmm = Pm[:, par * TB : (par + 1) * TB]
                                pgg = Pg[:, par * TB : (par + 1) * TB]
                                if kt in vd_done:
                                    ph.wait("tensor", "vd", vd_done[kt])
                                else:
                                    ph.wait("tensor", "sm", sm_done[kt])
                                    ph.wait("tensor", "sg", sg_done[kt])
                                if kt == 0:
                                    ph.wait("tensor", "evO", prev_evo)
                                vt = Vtk[:, kt * C + h * 128 : kt * C + (h + 1) * 128]
                                ph.run("tensor", lambda e, pmm=pmm, vt=vt, kt=kt, nkt=nkt: e.matmul(
                                    psOM[:, :], vt, pmm, start=(kt == 0), stop=(kt == nkt - 1)))
                                ph.run("tensor", lambda e, pmm=pmm, kt=kt, nkt=nkt: e.matmul(
                                    psLM[0:1, :], ones_cb[:, :], pmm,
                                    start=(kt == 0), stop=(kt == nkt - 1)))
                                ph.run("tensor", lambda e, pgg=pgg, vt=vt, kt=kt, nkt=nkt: e.matmul(
                                    psOG[:, :], vt, pgg, start=(kt == 0), stop=(kt == nkt - 1)))
                                ph.run("tensor", lambda e, pgg=pgg, kt=kt, nkt=nkt: e.matmul(
                                    psLG[0:1, :], ones_cb[:, :], pgg,
                                    start=(kt == 0), stop=(kt == nkt - 1)), sem="mmPV")
                                pv_done[kt] = ctx.val("mmPV")

                            dstage = [1 if deferred is not None else 6]

                            def adv(upto):
                                nonlocal deferred
                                while dstage[0] < upto:
                                    r = emit_mix_tail(deferred, dstage[0])
                                    if dstage[0] == 5:
                                        deferred = None
                                    else:
                                        deferred = r
                                    dstage[0] += 1

                            emit_S(0)
                            emit_S(1)
                            emit_S(2)
                            emit_PV(0)
                            for kt in range(3, nkt):
                                emit_S(kt)
                                if kt == 3:
                                    adv(3)
                                emit_PV(kt - 2)
                                if kt == 4:
                                    adv(6)
                            emit_PV(nkt - 2)
                            emit_PV(nkt - 1)
                            adv(6)

                            ph.wait("scalar", "voc", ctx.val("voc"))
                            ph.wait("scalar", "mmPV", ctx.val("mmPV"))
                            ph.run("scalar", lambda e, hp=hp: e.copy(
                                OMsb[:, hp * TB : (hp + 1) * TB], psOM[:, :]))
                            ph.run("scalar", lambda e, hp=hp: e.copy(
                                OGsb[:, hp * TB : (hp + 1) * TB], psOG[:, :]))
                            ph.run("scalar", lambda e, hp=hp: e.copy(
                                lrow[hp * 32 : hp * 32 + 1, 0:TB], psLM[0:1, :]))
                            ph.rund("scalar", lambda e, hp=hp: e.copy(
                                lrow[hp * 32 : hp * 32 + 1, TB : 2 * TB], psLG[0:1, :]),
                                sem="evO")
                            deferred = (h, hp, ctx.val("evO"))

                        for stg in range(5):
                            deferred = emit_mix_tail(deferred, stg)
                        emit_mix_tail(deferred, 5)
                        deferred = None
                        ph.wait("tensor", "voc", ctx.val("voc"))
                        for q in range(TB // 128):
                            if qb + q > 0:
                                ph.wait("tensor", "evA", ctx.val("evA"))
                            for mh in range(2):
                                for h2 in range(H):
                                    ph.run("tensor", lambda e, h2=h2, q=q, mh=mh: e.matmul(
                                        psS_[:, mh * TB : (mh + 1) * TB],
                                        Ocomb[:, h2 * TB + q * 128 : h2 * TB + q * 128 + 128],
                                        woT_sb[:, h2 * C + mh * 512 : h2 * C + (mh + 1) * 512],
                                        start=(h2 == 0), stop=(h2 == H - 1)),
                                        sem="mmA2" if h2 == H - 1 else None)
                            ph.wait("scalar", "mmA2", ctx.val("mmA2"))
                            tt = qb * (TB // 128) + q
                            for mh in range(2):
                                ph.run("scalar", lambda e, tt=tt, mh=mh: e.copy(
                                    zqA[:, tt * C + mh * 512 : tt * C + (mh + 1) * 512],
                                    psS_[:, mh * TB : (mh + 1) * TB]),
                                    sem="evA" if mh == 1 else None)
                        ph.wait("vector", "evA", ctx.val("evA"))
                    ph.wait("sync", "evA", ctx.val("evA"))
                    ph.build()
                qkv_st.close()

                ph = Phase(ctx, f"f_{s}")
                with contextlib.ExitStack() as pst:
                    Ep = pst.enter_context
                    xnb = Ep(nc.sbuf_tensor(f"xnb{s}", [128, NCH * TB], BF16))
                    dump = Ep(nc.sbuf_tensor(f"dumpF{s}", [128, 1024], BF16))
                    xnm = Ep(nc.sbuf_tensor(f"xnm{s}", [128, E * NCH * TB], BF16))
                    intr = Ep(nc.sbuf_tensor(f"intr{s}", [128, NIC * TB], BF16))
                    w1b = Ep(nc.sbuf_tensor(f"w1b{s}", [128, 2 * E * C], BF16))
                    wob = Ep(nc.sbuf_tensor(f"wob{s}", [128, 2 * C], BF16))
                    mskb = Ep(nc.sbuf_tensor(f"mskb{s}", [128, E * TB], BF16))
                    xfm2 = Ep(nc.sbuf_tensor(f"xfm2{s}", [128, 2 * TB], F32))
                    xtk2 = Ep(nc.sbuf_tensor(f"xtk2{s}", [128, C], F32))
                    xntk = Ep(nc.sbuf_tensor(f"xntk{s}", [128, C], F32))
                    ysb = Ep(nc.sbuf_tensor(f"ysb{s}", [128, C], F32))
                    fout = Ep(nc.sbuf_tensor(f"fout{s}", [128, C], F32))
                    b1_bf = Ep(nc.sbuf_tensor(f"b1bf{s}", [4, I], BF16))
                    grows4 = Ep(nc.sbuf_tensor(f"grows4{s}", [4, C], F32))
                    gam_b = Ep(nc.sbuf_tensor(f"gamb{s}", [128, C], F32))
                    bet_b = Ep(nc.sbuf_tensor(f"betb{s}", [128, C], F32))
                    bo_b = Ep(nc.sbuf_tensor(f"bob{s}", [128, C], F32))
                    psB2 = Ep(nc.psum_tensor(f"psB2{s}", [128, 2 * TB], F32))
                    psZ = Ep(nc.psum_tensor(f"psZ{s}", [128, 2 * TB], F32))
                    psY = Ep(nc.psum_tensor(f"psY{s}", [128, 2048], F32))
                    ph.dma("gpsimd", b1_bf[0:4, :], b1, "ldg")
                    ph.dma("sync", grows4[0:1, :], gam.rearrange("(a c) -> a c", a=1), "ldF")
                    ph.dma("sync", grows4[1:2, :], bet.rearrange("(a c) -> a c", a=1), "ldF")
                    ph.dma("sync", grows4[2:3, :], bo.rearrange("(a c) -> a c", a=1), "ldF")
                    ph.wait("tensor", "ldF", ctx.val("ldF"))
                    for j, dst in enumerate((gam_b, bet_b, bo_b)):
                        for mh in range(2):
                            ph.run("tensor", lambda e, j=j, mh=mh, psB2=psB2: e.matmul(
                                psB2[:, mh * TB : (mh + 1) * TB], sel4f[0:3, j * 128 : (j + 1) * 128],
                                grows4[0:3, mh * 512 : (mh + 1) * 512],
                                start=True, stop=True), sem="mmG")
                            ph.wait("scalar", "mmG", ctx.val("mmG"))
                            ph.run("scalar", lambda e, dst=dst, mh=mh, psB2=psB2: e.copy(
                                dst[:, mh * 512 : (mh + 1) * 512],
                                psB2[:, mh * TB : (mh + 1) * TB]), sem="evG")
                            ph.wait("tensor", "evG", ctx.val("evG"))
                    ph.wait("tensor", "ldg", ctx.val("ldg"))
                    stage_gxm = {}

                    def emit_stage(tb):
                        t0 = tb * TB
                        gtb = s * T + t0
                        if tb > 0:
                            ph.wait("tensor", "vxn", ctx.val("vxn"))
                        ph.run("tensor", lambda e, gtb=gtb, psB2=psB2: e.matmul(
                            psB2[:, 0:TB], sel4f[0:4, 0:128], srows4[0:4, gtb : gtb + TB],
                            start=True, stop=True))
                        ph.run("tensor", lambda e, gtb=gtb, psB2=psB2: e.matmul(
                            psB2[:, TB : 2 * TB], sel4f[0:4, 128:256],
                            srows4[0:4, gtb : gtb + TB], start=True, stop=True), sem="mmB2")
                        if tb > 0:
                            ph.wait("tensor", "sZ", ctx.val("sZ"))
                            ph.wait("vector", "gxm", ctx.val("gxm"))
                            ph.wait("vector", "mmZ", ctx.val("mmZ"))
                            ph.wait("gpsimd", "mmZ", ctx.val("mmZ"))
                            ph.wait("scalar", "gxm", ctx.val("gxm"))
                        for e_ in range(E):
                            pzh = psZ[:, (e_ % 2) * TB : (e_ % 2 + 1) * TB]
                            if e_ >= 2:
                                ph.wait("tensor", "evM", ctx.val("evM") - 1)
                            ph.run("tensor", lambda e, e_=e_, gtb=gtb, pzh=pzh: e.matmul(
                                pzh, sel4b[0:4, e_ * 128 : (e_ + 1) * 128],
                                masksr[0:4, gtb : gtb + TB],
                                start=True, stop=True), sem="mmM2")
                            ph.wait("scalar", "mmM2", ctx.val("mmM2"))
                            ph.run("scalar", lambda e, e_=e_, pzh=pzh: e.copy(
                                mskb[:, e_ * TB : (e_ + 1) * TB], pzh), sem="evM")
                        vldm = ctx.val("evM")
                        ph.wait("tensor", "evM", vldm)
                        ph.wait("vector", "mmB2", ctx.val("mmB2"))
                        ph.wait("vector", "evM", vldm)
                        ph.wait("gpsimd", "evM", vldm)
                        for a in range(NCH):
                            par = a % 2
                            if a >= 2:
                                ph.wait("sync", "vxn", ctx.val("vxn") - 1)
                            ph.dma("sync", xfm2[:, par * TB : (par + 1) * TB],
                                   x_fm[s, a * 128 : (a + 1) * 128, t0 : t0 + TB], f"ldX{par}")
                            ph.wait("vector", f"ldX{par}", ctx.val(f"ldX{par}"))
                            ph.run("vector", lambda e, par=par, psB2=psB2: e.tensor_mul(
                                xfm2[:, par * TB : (par + 1) * TB],
                                xfm2[:, par * TB : (par + 1) * TB], psB2[:, 0:TB]))
                            ph.run("vector", lambda e, par=par, a=a, psB2=psB2: e.tensor_sub(
                                xnb[:, a * TB : (a + 1) * TB],
                                xfm2[:, par * TB : (par + 1) * TB], psB2[:, TB : 2 * TB]),
                                sem="vxn")
                            ph.wait("gpsimd", "vxn", ctx.val("vxn"))
                            for e_ in range(2):
                                ph.run("gpsimd", lambda e, e_=e_, a=a: e.tensor_mul(
                                    xnm[:, (e_ * NCH + a) * TB : (e_ * NCH + a + 1) * TB],
                                    xnb[:, a * TB : (a + 1) * TB],
                                    mskb[:, e_ * TB : (e_ + 1) * TB]),
                                    sem="gxm" if e_ == 1 else None)
                            for e_ in range(2, 4):
                                ph.run("vector", lambda e, e_=e_, a=a: e.tensor_mul(
                                    xnm[:, (e_ * NCH + a) * TB : (e_ * NCH + a + 1) * TB],
                                    xnb[:, a * TB : (a + 1) * TB],
                                    mskb[:, e_ * TB : (e_ + 1) * TB]),
                                    sem="gxm" if e_ == 3 else None)
                        stage_gxm[tb] = ctx.val("gxm")

                    def emit_zsel(tb):
                        t0 = tb * TB
                        gtb = s * T + t0
                        for ic in range(NIC):
                            par = ic % 2
                            wsl = w1b[:, par * E * C : (par + 1) * E * C]
                            if ic >= 2:
                                ph.wait("sync", "mmZ", ctx.val("mmZ") - 1)
                            ph.dma("sync", wsl[:, :], w1r[ic, :, :], f"ldW1{par}")
                            if ic == 0:
                                ph.wait("tensor", "gxm", stage_gxm[tb])
                            ph.wait("tensor", f"ldW1{par}", ctx.val(f"ldW1{par}"))
                            if ic >= 2:
                                ph.wait("tensor", "sZ", ctx.val("sZ") - 1)
                            pz = psZ[:, par * TB : (par + 1) * TB]
                            first = True
                            for e_ in range(E):
                                for a in range(NCH):
                                    ph.run("tensor", lambda e, pz=pz, wsl=wsl, e_=e_, a=a,
                                           first=first: e.matmul(
                                        pz, wsl[:, e_ * C + a * 128 : e_ * C + a * 128 + 128],
                                        xnm[:, (e_ * NCH + a) * TB : (e_ * NCH + a + 1) * TB],
                                        start=first, stop=False))
                                    first = False
                            ph.run("tensor", lambda e, pz=pz, ic=ic, gtb=gtb: e.matmul(
                                pz, b1_bf[0:4, ic * 128 : (ic + 1) * 128],
                                masksr[0:4, gtb : gtb + TB], start=False, stop=True), sem="mmZ")
                            ph.wait("scalar", "mmZ", ctx.val("mmZ"))
                            ph.run("scalar", lambda e, pz=pz, ic=ic: e.activation(
                                intr[:, ic * TB : (ic + 1) * TB], pz, AF.Gelu), sem="sZ")

                    def emit_wo_epi(tb):
                        t0 = tb * TB
                        vsz = ctx.val("sZ")
                        ph.wait("tensor", "sZ", vsz)
                        for half in range(2):
                            for ic in range(NIC):
                                par = ic % 2
                                if ic >= 2:
                                    ph.wait("gpsimd", "mmYc", ctx.val("mmYc") - 1)
                                elif half == 1 or tb > 0:
                                    ph.wait("gpsimd", "mmYc", ctx.val("mmYc"))
                                ph.dma("gpsimd", wob[:, par * C : (par + 1) * C],
                                       wobf[ic * 128 : (ic + 1) * 128, :], f"ldWo{par}")
                                ph.wait("tensor", f"ldWo{par}", ctx.val(f"ldWo{par}"))
                                if ic == 0 and (half == 1 or tb > 0):
                                    ph.wait("tensor", "syf", ctx.val("syf"))
                                for q in range(2):
                                    tt_l = half * 2 + q
                                    for mh in range(2):
                                        py = psY[:, (q * 2 + mh) * 512 : (q * 2 + mh + 1) * 512]
                                        ph.run("tensor", lambda e, py=py, ic=ic, tt_l=tt_l,
                                               mh=mh, par=par: e.matmul(
                                            py, intr[:, ic * TB + tt_l * 128 : ic * TB + tt_l * 128 + 128],
                                            wob[:, par * C + mh * 512 : par * C + (mh + 1) * 512],
                                            start=(ic == 0), stop=(ic == NIC - 1)),
                                            sem="mmYc" if (q == 1 and mh == 1) else None)
                            vmy = ctx.val("mmYc")
                            ph.wait("scalar", "mmYc", vmy)
                            ph.wait("vector", "mmYc", vmy)
                            for q in range(2):
                                tt_l = half * 2 + q
                                tt = t0 // 128 + tt_l
                                k = s * NT + tt
                                xk = xtk2[:, 0:C]
                                ph.wait("scalar", "vout", ctx.val("vout"))
                                ph.wait("scalar", "stO", ctx.val("stO"))
                                ph.dma("scalar", xk, x_tok[s, tt * 128 : (tt + 1) * 128, :],
                                       "ldXT")
                                for mh in range(2):
                                    py = psY[:, (q * 2 + mh) * 512 : (q * 2 + mh + 1) * 512]
                                    ph.run("vector", lambda e, py=py, mh=mh: e.tensor_add(
                                        ysb[:, mh * 512 : (mh + 1) * 512], py,
                                        bo_b[:, mh * 512 : (mh + 1) * 512]),
                                        sem="syf" if mh == 1 else None)
                                ph.wait("vector", "ldXT", ctx.val("ldXT"))
                                ph.run("vector", lambda e, xk=xk, k=k: e.tensor_scalar(
                                    xntk[:, :], xk, stats[:, k : k + 1],
                                    stats[:, 128 + k : 129 + k], ALU.mult, ALU.add))
                                for mh in range(2):
                                    ph.run("vector", lambda e, mh=mh: e.tensor_add(
                                        ysb[:, mh * 512 : (mh + 1) * 512],
                                        ysb[:, mh * 512 : (mh + 1) * 512],
                                        xntk[:, mh * 512 : (mh + 1) * 512]))
                                ph.rund("vector", lambda e: e.reduce_sum(sc[:, 0:1], ysb[:, :],
                                                                         axis=AX.X))
                                ph.rund("vector", lambda e: e.scalar_tensor_tensor(
                                    dump[:, 0:C], ysb[:, :], 1.0, ysb[:, :], ALU.mult, ALU.mult,
                                    accum_out=sc[:, 1:2]))
                                ph.rund("vector", lambda e: e.tensor_scalar_mul(
                                    sc[:, 0:1], sc[:, 0:1], 1.0 / C))
                                ph.rund("vector", lambda e: e.tensor_scalar_mul(
                                    sc[:, 1:2], sc[:, 1:2], 1.0 / C))
                                ph.rund("vector", lambda e: e.tensor_mul(sc[:, 2:3], sc[:, 0:1],
                                                                         sc[:, 0:1]))
                                ph.rund("vector", lambda e: e.tensor_sub(sc[:, 2:3], sc[:, 1:2],
                                                                         sc[:, 2:3]), sem="vy")
                                ph.wait("scalar", "vy", ctx.val("vy"))
                                ph.rund("scalar", lambda e: e.activation(
                                    sc[:, 5:6], sc[:, 2:3], AF.Sqrt, bias=1e-12), sem="sy0")
                                ph.wait("vector", "sy0", ctx.val("sy0"))
                                ph.rund("vector", lambda e: e.reciprocal(sc[:, 3:4], sc[:, 5:6]))
                                ph.rund("vector", lambda e: e.tensor_mul(sc[:, 4:5], sc[:, 0:1],
                                                                         sc[:, 3:4]))
                                ph.rund("vector", lambda e: e.tensor_scalar_mul(
                                    sc[:, 4:5], sc[:, 4:5], -1.0))
                                ph.run("vector", lambda e: e.tensor_scalar(
                                    fout[:, :], ysb[:, :], sc[:, 3:4], sc[:, 4:5],
                                    ALU.mult, ALU.add))
                                ph.run("vector", lambda e: e.tensor_mul(fout[:, :], fout[:, :],
                                                                        gam_b[:, :]))
                                ph.run("vector", lambda e: e.tensor_add(fout[:, :], fout[:, :],
                                                                        bet_b[:, :]))
                                ph.run("vector", lambda e, xk=xk: e.tensor_add(
                                    fout[:, :], fout[:, :], xk))
                                ph.run("vector", lambda e: e.tensor_add(fout[:, :], fout[:, :],
                                                                        xntk[:, :]))
                                ph.run("vector", lambda e, tt=tt: e.tensor_add(
                                    fout[:, :], fout[:, :], zqA[:, tt * C : (tt + 1) * C]),
                                    sem="vout")
                                ph.wait("sync", "vout", ctx.val("vout"))
                                ph.dma("sync", out[s, tt * 128 : (tt + 1) * 128, :],
                                       fout[:, :], "stO")

                    emit_stage(0)
                    for tb in range(NTB):
                        emit_zsel(tb)
                        if tb + 1 < NTB:
                            emit_stage(tb + 1)
                        emit_wo_epi(tb)
                    ph.wait("sync", "stO", ctx.val("stO"))
                    ph.build()

        if __debug__ and globals().get("ENDDUMP"):
            ph = Phase(ctx, "enddump")
            for nm, ap_, shp, dt_ in (("stats", stats[:, :], [128, 224], F32),
                                      ("srows4", srows4[0:4, :], [4, STW], F32),
                                      ("masksr", masksr[0:4, :], [4, SPC * T], BF16),
                                      ("kp", kp_all[:, :], [128, H * NT], F32)):
                t = nc.dram_tensor("dbg_" + nm, shp, dt_, kind="ExternalOutput").ap()
                ph.dma("sync", t, ap_, "dbgE")
            ph.wait("sync", "dbgE", ctx.val("dbgE"))
            ph.build()
    return nc


def _host_constants():
    inv = 1.0 / (10000.0 ** (np.arange(0, D, 2, dtype=np.float64) / D))
    t = np.arange(T, dtype=np.float64)
    fr = np.outer(inv, t)
    cos = np.concatenate([np.cos(fr), np.cos(fr)], axis=0)
    sinm = np.concatenate([np.sin(fr), np.sin(fr)], axis=0)
    rot = np.zeros((128, 128), np.float32)
    rot[:64, 64:] = -np.eye(64, dtype=np.float32)
    rot[64:, :64] = np.eye(64, dtype=np.float32)
    tri = (np.arange(128)[:, None] <= np.arange(128)[None, :]).astype(ml_dtypes.bfloat16)
    ident = np.eye(128)
    sel4 = np.zeros((4, 512), np.float32)
    for j in range(4):
        sel4[j, j * 128 : (j + 1) * 128] = 1.0
    return (cos.astype(np.float32), sinm.astype(np.float32), tri,
            ident.astype(ml_dtypes.bfloat16), ident.astype(np.float32),
            np.ones((1, 128), np.float32), sel4,
            np.ascontiguousarray(rot.T).astype(ml_dtypes.bfloat16))


def kernel(L0_attn_x, L0_attn_Wq, L0_attn_Wk, L0_attn_Wv, L0_attn_Wo,
           L0_attn_ln_g, L0_attn_ln_b, L0_attn_head_scales, L0_attn_head_dirs,
           L0_ffn_W_router, L0_ffn_W1, L0_ffn_b1, L0_ffn_Wo, L0_ffn_bo,
           L0_ffn_ln_gamma, L0_ffn_ln_beta):
    x = np.asarray(L0_attn_x, np.float32)
    asc = np.ascontiguousarray
    cos, sinm, tri, id_b, id_f, ones, sel4, rswap = _host_constants()
    shared = {
        "wqT": asc(np.asarray(L0_attn_Wq, np.float32).T),
        "wkT": asc(np.asarray(L0_attn_Wk, np.float32).T),
        "wvT": asc(np.asarray(L0_attn_Wv, np.float32).T),
        "woT": asc(np.asarray(L0_attn_Wo, np.float32).T),
        "ln_g": np.asarray(L0_attn_ln_g, np.float32),
        "ln_b": np.asarray(L0_attn_ln_b, np.float32),
        "hscale": np.asarray(L0_attn_head_scales, np.float32),
        "dirsT": asc(np.asarray(L0_attn_head_dirs, np.float32).T),
        "wr": np.asarray(L0_ffn_W_router, np.float32),
        "w1": np.asarray(L0_ffn_W1, np.float32),
        "b1": np.asarray(L0_ffn_b1, np.float32),
        "wo2": np.asarray(L0_ffn_Wo, np.float32),
        "bo": np.asarray(L0_ffn_bo, np.float32),
        "gam": np.asarray(L0_ffn_ln_gamma, np.float32),
        "bet": np.asarray(L0_ffn_ln_beta, np.float32),
        "ropec": cos, "ropes": sinm, "tri_c": tri,
        "ident_b": id_b, "ident_f": id_f, "ones_in": ones,
        "sel4f_in": sel4, "sel4b_in": sel4.astype(ml_dtypes.bfloat16),
        "rswapT": rswap,
    }
    if "nc" not in _CACHE:
        _CACHE["nc"] = build_program()
    nc = _CACHE["nc"]
    in_maps = []
    for i in range(NCORE):
        xs = x[i * SPC : (i + 1) * SPC]
        m = dict(shared)
        m["x_tok"] = asc(xs)
        m["x_fm"] = asc(xs.transpose(0, 2, 1))
        in_maps.append(m)
    res = run_bass_kernel_spmd(nc, in_maps, core_ids=list(range(NCORE)))
    return np.concatenate([r["out"] for r in res.results], axis=0)
